# version 01
# baseline (speedup 1.0000x reference)
import numpy as np
import concourse.bass as bass
import concourse.mybir as mybir

F32 = mybir.dt.float32
BF16 = mybir.dt.bfloat16
F16 = mybir.dt.float16
AF = mybir.ActivationFunctionType
ALU = mybir.AluOpType
AX = mybir.AxisListType

D = 1024
T = 4096
ND = D // 128

ENGINES = ("sync", "gpsimd", "tensor", "vector", "scalar")
ENG_SEM = {"sync": "q_sy", "gpsimd": "q_gq", "tensor": "pe", "vector": "ve", "scalar": "se"}


class Sem:
    __slots__ = ("h", "name", "n")
    def __init__(self, nc, name):
        self.h = nc.alloc_semaphore(name)
        self.name = name
        self.n = 0


class Buf:
    __slots__ = ("name", "w", "r")
    def __init__(self, name):
        self.name = name
        self.w = None
        self.r = []

    def reset(self):
        self.w = None
        self.r = []


class _Instr:
    __slots__ = ("proxy", "instr")
    def __init__(self, proxy, instr):
        self.proxy = proxy
        self.instr = instr

    def then_inc(self, sem, n=1):
        if self.instr is not None:
            self.instr.then_inc(sem.h, n)
        sem.n += n
        env = self.proxy._env
        if env.counting:
            ck = dict(self.proxy.clock)
            ck[sem.name] = max(ck.get(sem.name, 0), sem.n)
            env.prog.snap[(sem.name, sem.n)] = ck
        return (sem, sem.n)


class EngProxy:
    def __init__(self, env, name, real_eng):
        self._env = env
        self._name = name
        self._real = real_eng
        self.clock = {}

    def _learn(self, tok):
        env = self._env
        if not env.counting:
            return
        sem, val = tok
        snap = env.prog.snap.get((sem.name, val))
        assert snap is not None, f"wait on unknown token {sem.name}={val}"
        for k, v in snap.items():
            if self.clock.get(k, 0) < v:
                self.clock[k] = v

    def covers(self, tok):
        sem, val = tok
        return self.clock.get(sem.name, 0) >= val

    def __getattr__(self, attr):
        env = self._env
        if self._real is not None:
            meth = getattr(self._real, attr)
            def call(*a, **kw):
                env.n_inst[self._name] += 1
                return _Instr(self, meth(*a, **kw))
        else:
            def call(*a, **kw):
                env.n_inst[self._name] += 1
                return _Instr(self, None)
        return call


class Env:
    def __init__(self, prog, real_name, real_eng, counting=False):
        self.prog = prog
        self.real_name = real_name
        self.counting = counting
        self.n_inst = {n: 0 for n in ENGINES}
        self.n_waits = {n: 0 for n in ENGINES}
        for name in ENGINES:
            setattr(self, name, EngProxy(self, name, real_eng if name == real_name else None))
        self._waited = {}

    def wait_tok(self, proxy, tok):
        if tok is None:
            return
        sem, val = tok
        if val <= 0:
            return
        proxy._learn(tok)
        key = (proxy._name, sem.name)
        if self._waited.get(key, -1) >= val:
            return
        self._waited[key] = val
        self.n_waits[proxy._name] += 1
        if proxy._real is not None:
            proxy._real.wait_ge(sem.h, val)

    def group(self, proxy, reads=(), writes=(), extra=()):
        return _Group(self, proxy, list(reads), list(writes), list(extra))


class _Group:
    def __init__(self, env, proxy, reads, writes, extra):
        self.env = env
        self.proxy = proxy
        self.reads = reads
        self.writes = writes
        self.extra = extra
        self.last = None
        self.tok = None

    def __enter__(self):
        env, proxy = self.env, self.proxy
        for t in self.extra:
            env.wait_tok(proxy, t)
        for b in self.reads:
            env.wait_tok(proxy, b.w)
        for b in self.writes:
            env.wait_tok(proxy, b.w)
            for t in b.r:
                env.wait_tok(proxy, t)
        return self

    def __exit__(self, et, ev, tb):
        if et is not None:
            return False
        assert self.last is not None, "group must set g.last"
        sem = self.env.prog.sem(ENG_SEM[self.proxy._name])
        self.tok = self.last.then_inc(sem, 1)
        for b in self.reads:
            b.r.append(self.tok)
        for b in self.writes:
            b.w = self.tok
            b.r = []
        return False


class DQ:
    def __init__(self, prog, proxy_name, chan_name):
        self.prog = prog
        self.proxy_name = proxy_name
        self.name = chan_name
        self.sem = prog.sem(chan_name)

    @property
    def last_tok(self):
        return self.prog.dq_last.get(self.name)

    @last_tok.setter
    def last_tok(self, v):
        self.prog.dq_last[self.name] = v

    def group(self, env, reads=(), writes=(), extra=()):
        return _DmaGroup(self, env, list(reads), list(writes), list(extra))


class _DmaGroup:
    def __init__(self, dq, env, reads, writes, extra):
        self.dq = dq
        self.env = env
        self.proxy = getattr(env, dq.proxy_name)
        self.reads = reads
        self.writes = writes
        self.extra = extra
        self.tok = None

    def __enter__(self):
        env, proxy = self.env, self.proxy
        for t in self.extra:
            env.wait_tok(proxy, t)
        for b in self.reads:
            env.wait_tok(proxy, b.w)
        for b in self.writes:
            env.wait_tok(proxy, b.w)
            for t in b.r:
                env.wait_tok(proxy, t)
        if env.counting:
            lt = self.dq.last_tok
            if lt is not None:
                assert proxy.covers(lt), (
                    f"DMA channel {self.dq.name} reissued before issuing engine "
                    f"knows previous group completed (token {lt[0].name}={lt[1]})")
        return self

    def dma(self, out, in_, **kw):
        instr = self.proxy.dma_start(out=out, in_=in_, **kw)
        instr.then_inc(self.dq.sem, 16)
        return instr

    def __exit__(self, et, ev, tb):
        if et is not None:
            return False
        self.tok = (self.dq.sem, self.dq.sem.n)
        self.dq.last_tok = self.tok
        for b in self.reads:
            b.r.append(self.tok)
        for b in self.writes:
            b.w = self.tok
            b.r = []
        return False


class Prog:
    def __init__(self, nc):
        self.nc = nc
        self.sems = {}
        self.bufs = {}
        self.snap = {}
        self.dq_last = {}

    def sem(self, name):
        if name not in self.sems:
            self.sems[name] = Sem(self.nc, name)
        return self.sems[name]

    def buf(self, name):
        if name not in self.bufs:
            self.bufs[name] = Buf(name)
        return self.bufs[name]

    def reset(self):
        for s in self.sems.values():
            s.n = 0
        for b in self.bufs.values():
            b.reset()
        self.dq_last = {}

    def run(self, orchestrate):
        self.reset()
        self.snap = {}
        env0 = Env(self, None, None, counting=True)
        orchestrate(env0)
        counts = dict(env0.n_inst)
        waits = dict(env0.n_waits)
        with self.nc.Block() as b:
            for name in ENGINES:
                def body(eng, name=name):
                    self.reset()
                    env = Env(self, name, eng)
                    orchestrate(env)
                getattr(b, name)(body)
        return counts, waits


def halloc(nc, name, shape, dtype):
    return nc.alloc_sbuf_tensor(name, shape, dtype).ap()


def palloc(nc, name, cols=512, dtype=F32):
    return nc.alloc_psum_tensor(name, [128, cols], dtype).ap()


def ceil_div(a, b):
    return (a + b - 1) // b


"""L0 attn phase: y = x + inner(ln(x)); inner = GQA linear attention (ELU+1 feat,
low-rank kv bottleneck, chunked causal state), Wo proj, inner residual, affine LN.
Feature-major. Matmul operands base-aligned via head-duplicated We layout."""

TT = 512
NTT = T // TT
CH = 128
NCH = TT // CH
HKV = 4
EPS = 1e-6
STAGE = 99
SUBSTAGE = 99


def emit_ln_generic(env, te, ve, se, gq, b, aps, src, dst, n=TT, eps=1e-5,
                    gamma=None, beta=None):
    xbf, rowa = aps["xbf"], aps["rowa"]
    rstd_b, nmr_b = aps["rstd_b"], aps["nmr_b"]
    ones_sb, onesk1_sb, negk1_sb = aps["ones_sb"], aps["onesk1_sb"], aps["negk1_sb"]
    ps = aps["ps"]
    b_src, b_dst = b["src"], b["dst"]

    with env.group(ve, reads=[b_src], writes=[b["xbf"]]) as g:
        for gi in range(ND):
            g.last = ve.tensor_copy(xbf[:, gi * n:(gi + 1) * n], src(gi))
    with env.group(te, reads=[b["xbf"], b["consts"]], writes=[b["ps0"]]) as g:
        for gi in range(ND):
            g.last = te.matmul(ps[0][0:1, 0:n], ones_sb[:, :], xbf[:, gi * n:(gi + 1) * n],
                               start=(gi == 0), stop=(gi == ND - 1))
    with env.group(ve, writes=[b["xbf"]]) as g:
        for gi in range(ND):
            g.last = ve.tensor_mul(xbf[:, gi * n:(gi + 1) * n], xbf[:, gi * n:(gi + 1) * n],
                                   xbf[:, gi * n:(gi + 1) * n])
    with env.group(te, reads=[b["xbf"], b["consts"]], writes=[b["ps1"]]) as g:
        for gi in range(ND):
            g.last = te.matmul(ps[1][0:1, 0:n], ones_sb[:, :], xbf[:, gi * n:(gi + 1) * n],
                               start=(gi == 0), stop=(gi == ND - 1))
    with env.group(ve, reads=[b["ps0"], b["ps1"]], writes=[b["rowa"]]) as g:
        ve.tensor_scalar(rowa[0:1, 0:n], ps[0][0:1, 0:n], 1.0 / D, None, ALU.mult)
        g.last = ve.tensor_scalar(rowa[0:1, n:2 * n], ps[1][0:1, 0:n], 1.0 / D, eps, ALU.mult, ALU.add)
    with env.group(ve, writes=[b["rowa2"]], reads=[b["rowa"]]) as g:
        g.last = ve.tensor_mul(rowa[0:1, 2 * n:3 * n], rowa[0:1, 0:n], rowa[0:1, 0:n])
    with env.group(ve, reads=[b["rowa2"]], writes=[b["rowa"]]) as g:
        g.last = ve.tensor_tensor(rowa[0:1, n:2 * n], rowa[0:1, n:2 * n],
                                  rowa[0:1, 2 * n:3 * n], ALU.subtract)
    with env.group(se, reads=[b["rowa"]], writes=[b["rowa2"]]) as g:
        g.last = se.activation(rowa[0:1, 2 * n:3 * n], rowa[0:1, n:2 * n], AF.Sqrt)
    with env.group(ve, reads=[b["rowa2"]], writes=[b["rowa"]]) as g:
        g.last = ve.reciprocal(rowa[0:1, n:2 * n], rowa[0:1, 2 * n:3 * n])
    with env.group(ve, reads=[b["rowa"]], writes=[b["rowa2"]]) as g:
        g.last = ve.tensor_mul(rowa[0:1, 2 * n:3 * n], rowa[0:1, 0:n], rowa[0:1, n:2 * n])
    with env.group(te, reads=[b["rowa"], b["rowa2"], b["consts"]], writes=[b["ps2"], b["ps3"]]) as g:
        te.matmul(ps[2][:, 0:n], onesk1_sb[0:1, :], rowa[0:1, n:2 * n], start=True, stop=True)
        g.last = te.matmul(ps[3][:, 0:n], negk1_sb[0:1, :], rowa[0:1, 2 * n:3 * n], start=True, stop=True)
    with env.group(ve, reads=[b["ps2"], b["ps3"]], writes=[b["rstd_b"], b["nmr_b"]]) as g:
        ve.tensor_copy(rstd_b[:, :n], ps[2][:, 0:n])
        g.last = ve.tensor_copy(nmr_b[:, :n], ps[3][:, 0:n])
    with env.group(ve, reads=[b_src, b["rstd_b"]], writes=[b_dst]) as g:
        for gi in range(ND):
            g.last = ve.tensor_tensor(dst(gi), src(gi), rstd_b[:, :n], ALU.mult)
    with env.group(gq, reads=[b["nmr_b"]], writes=[b_dst]) as g:
        for gi in range(ND):
            g.last = gq.tensor_tensor(dst(gi), dst(gi), nmr_b[:, :n], ALU.add)
    if gamma is not None:
        with env.group(ve, reads=[b["consts2"]], writes=[b_dst]) as g:
            for gi in range(ND):
                g.last = ve.tensor_scalar(dst(gi), dst(gi), gamma[:, gi:gi + 1],
                                          beta[:, gi:gi + 1], ALU.mult, ALU.add)


def build_l0attn():
    nc = bass.Bass("TRN2", target_bir_lowering=False)
    x = nc.dram_tensor("x", [D, T], F32, kind="ExternalInput").ap()
    Wq = nc.dram_tensor("Wq", [D, D], F32, kind="ExternalInput").ap()
    Wk = nc.dram_tensor("Wk", [D, 256], F32, kind="ExternalInput").ap()
    Wv = nc.dram_tensor("Wv", [D, 256], F32, kind="ExternalInput").ap()
    Wc = nc.dram_tensor("Wc", [256, 64], F32, kind="ExternalInput").ap()
    WeDup = nc.dram_tensor("WeDup", [64, 512], F32, kind="ExternalInput").ap()
    Wo = nc.dram_tensor("Wo", [D, D], F32, kind="ExternalInput").ap()
    bq = nc.dram_tensor("bq", [128, ND], F32, kind="ExternalInput").ap()
    bk = nc.dram_tensor("bk", [128, 2], F32, kind="ExternalInput").ap()
    bv = nc.dram_tensor("bv", [128, 2], F32, kind="ExternalInput").ap()
    bo = nc.dram_tensor("bo", [128, ND], F32, kind="ExternalInput").ap()
    gamma = nc.dram_tensor("gamma", [128, ND], F32, kind="ExternalInput").ap()
    beta = nc.dram_tensor("beta", [128, ND], F32, kind="ExternalInput").ap()
    ones128 = nc.dram_tensor("ones128", [128, 1], BF16, kind="ExternalInput").ap()
    onesk1f = nc.dram_tensor("onesk1f", [1, 128], F32, kind="ExternalInput").ap()
    negk1f = nc.dram_tensor("negk1f", [1, 128], F32, kind="ExternalInput").ap()
    onesk1b = nc.dram_tensor("onesk1b", [128, 128], BF16, kind="ExternalInput").ap()
    trimask = nc.dram_tensor("trimask", [128, 4 * CH], BF16, kind="ExternalInput").ap()
    y = nc.dram_tensor("y", [D, T], F32, kind="ExternalOutput").ap()

    NS = 2
    xs = halloc(nc, "xs", [128, NS * ND * TT], F32)
    lnb = halloc(nc, "lnb", [128, NS * ND * TT], BF16)
    xbf = halloc(nc, "xbf", [128, ND * TT], BF16)
    qf = halloc(nc, "qf", [128, NS * ND * TT], BF16)
    ob = halloc(nc, "ob", [128, NS * ND * TT], BF16)
    yb = halloc(nc, "yb", [128, ND * TT], F32)
    k1 = halloc(nc, "k1", [128, 2 * TT], BF16)
    v1 = halloc(nc, "v1", [128, 2 * TT], BF16)
    kc_sb = halloc(nc, "kc_sb", [64, TT], BF16)
    vc_sb = halloc(nc, "vc_sb", [64, TT], BF16)
    kf_dup = halloc(nc, "kf_dup", [128, 4 * TT], BF16)
    kf_t = halloc(nc, "kf_t", [128, NCH * 512], BF16)
    ve_t = halloc(nc, "ve_t", [128, NCH * 512], BF16)
    am = halloc(nc, "am", [128, 4 * CH], BF16)
    ssb = halloc(nc, "ssb", [128, 4 * 66], BF16)
    ssf = halloc(nc, "ssf", [128, 4 * 66], F32)
    rrow = halloc(nc, "rrow", [128, 4 * CH], F32)
    rrb = halloc(nc, "rrb", [128, 2 * CH], BF16)
    rb_sb = halloc(nc, "rb_sb", [128, 2 * CH], BF16)
    ft = halloc(nc, "ft", [128, 2 * TT], F32)
    rowa = halloc(nc, "rowa", [1, 3 * TT], F32)
    rstd_b = halloc(nc, "rstd_b", [128, TT], F32)
    nmr_b = halloc(nc, "nmr_b", [128, TT], F32)
    wq_sb = halloc(nc, "wq_sb", [128, ND * D], BF16)
    wo_sb = halloc(nc, "wo_sb", [128, ND * D], BF16)
    wk_sb = halloc(nc, "wk_sb", [128, ND * 256], BF16)
    wv_sb = halloc(nc, "wv_sb", [128, ND * 256], BF16)
    wc_sb = halloc(nc, "wc_sb", [128, 2 * 64], BF16)
    we_sb = halloc(nc, "we_sb", [64, 512], BF16)
    bq_sb = halloc(nc, "bq_sb", [128, ND], F32)
    nbq_sb = halloc(nc, "nbq_sb", [128, ND], F32)
    bk_sb = halloc(nc, "bk_sb", [128, 2], F32)
    bv_sb = halloc(nc, "bv_sb", [128, 2], F32)
    bo_sb = halloc(nc, "bo_sb", [128, ND], F32)
    gam_sb = halloc(nc, "gam_sb", [128, ND], F32)
    bet_sb = halloc(nc, "bet_sb", [128, ND], F32)
    ones_sb = halloc(nc, "ones_sb", [128, 1], BF16)
    onesk1_sb = halloc(nc, "onesk1_sb", [1, 128], F32)
    negk1_sb = halloc(nc, "negk1_sb", [1, 128], F32)
    onesk1b_sb = halloc(nc, "onesk1b_sb", [128, 128], BF16)
    ones_sb64 = halloc(nc, "ones_sb64", [128, 1], BF16)
    tri_sb = halloc(nc, "tri_sb", [128, 4 * CH], BF16)

    ps = [palloc(nc, f"ps{i}", 512) for i in range(8)]

    p = Prog(nc)

    def orchestrate(env):
        gq, te, ve, se = env.gpsimd, env.tensor, env.vector, env.scalar
        qw = DQ(p, "gpsimd", "qw")
        qx = [DQ(p, "sync", f"qx{i}") for i in range(NS)]
        qs = [DQ(p, "sync", f"qs{i}") for i in range(NS)]
        qc = DQ(p, "sync", "qc")
        B = p.buf
        b_w = B("w"); b_consts = B("consts"); b_consts2 = B("consts2")
        b_xs = [B(f"xs{i}") for i in range(NS)]
        b_ln = [B(f"ln{i}") for i in range(NS)]
        b_qf = [B(f"qf{i}") for i in range(NS)]
        b_ob = [B(f"ob{i}") for i in range(NS)]
        b_yb = B("yb")
        b_xbf = B("xbf"); b_rowa = B("rowa"); b_rowa2 = B("rowa2")
        b_rstdb = B("rstdb"); b_nmrb = B("nmrb")
        b_k1 = B("k1"); b_v1 = B("v1"); b_kc = B("kc"); b_vc = B("vc")
        b_kfd = B("kfd"); b_kft = B("kft"); b_vet = B("vet")
        b_am = B("am"); b_ssb = B("ssb"); b_ssf = B("ssf"); b_rrow = B("rrow"); b_rrb = B("rrb")
        b_rbsb = B("rbsb"); b_ft = B("ft")
        b_ps = [B(f"ps{i}") for i in range(8)]

        with qw.group(env, writes=[b_w]) as g:
            for c in range(ND):
                g.dma(wq_sb[:, c * D:(c + 1) * D], Wq[c * 128:(c + 1) * 128, :])
                g.dma(wo_sb[:, c * D:(c + 1) * D], Wo[c * 128:(c + 1) * 128, :])
                g.dma(wk_sb[:, c * 256:(c + 1) * 256], Wk[c * 128:(c + 1) * 128, :])
                g.dma(wv_sb[:, c * 256:(c + 1) * 256], Wv[c * 128:(c + 1) * 128, :])
            g.dma(wc_sb[:, 0:64], Wc[0:128, :])
            g.dma(wc_sb[:, 64:128], Wc[128:256, :])
            g.dma(we_sb[:, :], WeDup[:, :])
        with qc.group(env, writes=[b_consts, b_consts2]) as g:
            g.dma(ones_sb[:, :], ones128[:, :])
            g.dma(ones_sb64[:, :], ones128[:, :])
            g.dma(onesk1b_sb[:, :], onesk1b[:, :])
            g.dma(tri_sb[:, :], trimask[:, :])
            g.dma(onesk1_sb[:, :], onesk1f[:, :])
            g.dma(negk1_sb[:, :], negk1f[:, :])
            g.dma(bq_sb[:, :], bq[:, :])
            g.dma(bk_sb[:, :], bk[:, :])
            g.dma(bv_sb[:, :], bv[:, :])
            g.dma(bo_sb[:, :], bo[:, :])
            g.dma(gam_sb[:, :], gamma[:, :])
            g.dma(bet_sb[:, :], beta[:, :])
        with env.group(ve, reads=[b_consts2], writes=[b_consts2]) as g:
            g.last = ve.tensor_scalar(nbq_sb[:, :], bq_sb[:, :], -1.0, None, ALU.mult)
        with env.group(ve, writes=[b_ssf]) as g:
            g.last = ve.memset(ssf[:, :], 0.0)

        for k in range(NTT):
            slot = k % NS
            xo = slot * ND * TT
            t0 = k * TT

            with qx[slot].group(env, writes=[b_xs[slot]]) as g:
                for gi in range(ND):
                    g.dma(xs[:, xo + gi * TT:xo + (gi + 1) * TT],
                          x[gi * 128:(gi + 1) * 128, t0:t0 + TT])
            b = {"src": b_xs[slot], "xbf": b_xbf, "rowa": b_rowa, "rowa2": b_rowa2,
                 "rstd_b": b_rstdb, "nmr_b": b_nmrb, "dst": b_ln[slot], "consts": b_consts,
                 "ps0": b_ps[0], "ps1": b_ps[1], "ps2": b_ps[2], "ps3": b_ps[3]}
            aps = {"xbf": xbf, "rowa": rowa, "rstd_b": rstd_b, "nmr_b": nmr_b,
                   "ones_sb": ones_sb, "onesk1_sb": onesk1_sb, "negk1_sb": negk1_sb, "ps": ps}
            emit_ln_generic(env, te, ve, se, gq, b, aps,
                            lambda gi: xs[:, xo + gi * TT:xo + (gi + 1) * TT],
                            lambda gi: lnb[:, xo + gi * TT:xo + (gi + 1) * TT])

            if STAGE >= 2:
                for go in range(ND):
                    bk_ = (go % 2)
                    with env.group(te, reads=[b_ln[slot], b_w], writes=[b_ps[bk_]]) as g:
                        for c in range(ND):
                            g.last = te.matmul(ps[bk_][:, 0:TT], wq_sb[:, c * D + go * 128:c * D + (go + 1) * 128],
                                               lnb[:, xo + c * TT:xo + (c + 1) * TT],
                                               start=(c == 0), stop=(c == ND - 1))
                    with env.group(se, reads=[b_ps[bk_], b_consts2], writes=[b_ft]) as g:
                        se.activation(ft[:, 0:TT], ps[bk_][:, 0:TT], AF.Relu, bias=bq_sb[:, go:go + 1])
                        g.last = se.activation(ft[:, TT:2 * TT], ps[bk_][:, 0:TT], AF.Relu,
                                               bias=nbq_sb[:, go:go + 1], scale=-1.0)
                    with env.group(se, reads=[b_ft], writes=[b_ps[bk_]]) as g:
                        g.last = se.activation(ps[bk_][:, 0:TT], ft[:, TT:2 * TT], AF.Exp, scale=-1.0)
                    with env.group(ve, reads=[b_ps[bk_], b_ft], writes=[b_qf[slot]]) as g:
                        g.last = ve.tensor_tensor(qf[:, xo + go * TT:xo + (go + 1) * TT],
                                                  ps[bk_][:, 0:TT], ft[:, 0:TT], ALU.add)

            if STAGE >= 3:
                for (wsb, bias_sb, dst, bdst) in ((wk_sb, bk_sb, k1, b_k1), (wv_sb, bv_sb, v1, b_v1)):
                    for go in range(2):
                        bk_ = (go % 2)
                        with env.group(te, reads=[b_ln[slot], b_w], writes=[b_ps[bk_]]) as g:
                            for c in range(ND):
                                g.last = te.matmul(ps[bk_][:, 0:TT],
                                                   wsb[:, c * 256 + go * 128:c * 256 + (go + 1) * 128],
                                                   lnb[:, xo + c * TT:xo + (c + 1) * TT],
                                                   start=(c == 0), stop=(c == ND - 1))
                        with env.group(ve, reads=[b_ps[bk_], b_consts2], writes=[bdst]) as g:
                            g.last = ve.tensor_scalar(dst[:, go * TT:(go + 1) * TT], ps[bk_][:, 0:TT],
                                                      bias_sb[:, go:go + 1], None, ALU.add)

            if STAGE >= 3:
                with env.group(te, reads=[b_k1, b_w], writes=[b_ps[0]]) as g:
                    for c in range(2):
                        g.last = te.matmul(ps[0][0:64, 0:TT], wc_sb[:, c * 64:(c + 1) * 64],
                                           k1[:, c * TT:(c + 1) * TT], start=(c == 0), stop=(c == 1))
                with env.group(te, reads=[b_v1, b_w], writes=[b_ps[1]]) as g:
                    for c in range(2):
                        g.last = te.matmul(ps[1][0:64, 0:TT], wc_sb[:, c * 64:(c + 1) * 64],
                                           v1[:, c * TT:(c + 1) * TT], start=(c == 0), stop=(c == 1))
                with env.group(ve, reads=[b_ps[0]], writes=[b_kc]) as g:
                    g.last = ve.tensor_copy(kc_sb[0:64, 0:TT], ps[0][0:64, 0:TT])
                with env.group(ve, reads=[b_ps[1]], writes=[b_vc]) as g:
                    g.last = ve.tensor_copy(vc_sb[0:64, 0:TT], ps[1][0:64, 0:TT])

            if STAGE >= 4:
                for g_kv in range(HKV):
                    bk_ = (g_kv % 2)
                    with env.group(te, reads=[b_kc, b_w], writes=[b_ps[bk_]]) as g:
                        g.last = te.matmul(ps[bk_][:, 0:TT], we_sb[0:64, g_kv * 128:(g_kv + 1) * 128],
                                           kc_sb[0:64, 0:TT], start=True, stop=True)
                    with env.group(se, reads=[b_ps[bk_]], writes=[b_ft]) as g:
                        se.activation(ft[:, 0:TT], ps[bk_][:, 0:TT], AF.Relu)
                        g.last = se.activation(ft[:, TT:2 * TT], ps[bk_][:, 0:TT], AF.Relu, scale=-1.0)
                    with env.group(se, reads=[b_ft], writes=[b_ps[bk_]]) as g:
                        g.last = se.activation(ps[bk_][:, 0:TT], ft[:, TT:2 * TT], AF.Exp, scale=-1.0)
                    with env.group(ve, reads=[b_ps[bk_], b_ft], writes=[b_kfd]) as g:
                        g.last = ve.tensor_tensor(kf_dup[:, g_kv * TT:(g_kv + 1) * TT],
                                                  ps[bk_][:, 0:TT], ft[:, 0:TT], ALU.add)

            if STAGE >= 5:
                for c in range(NCH):
                    with env.group(te, reads=[b_kc, b_w], writes=[b_ps[2]]) as g:
                        g.last = te.matmul(ps[2][:, 0:512], kc_sb[0:64, c * CH:(c + 1) * CH],
                                           we_sb[0:64, :], start=True, stop=True)
                    with env.group(se, reads=[b_ps[2]], writes=[b_ft]) as g:
                        se.activation(ft[:, 0:512], ps[2][:, 0:512], AF.Relu)
                        g.last = se.activation(ft[:, 512:1024], ps[2][:, 0:512], AF.Relu, scale=-1.0)
                    with env.group(se, reads=[b_ft], writes=[b_ps[2]]) as g:
                        g.last = se.activation(ps[2][:, 0:512], ft[:, 512:1024], AF.Exp, scale=-1.0)
                    with env.group(ve, reads=[b_ps[2], b_ft], writes=[b_kft]) as g:
                        g.last = ve.tensor_tensor(kf_t[:, c * 512:(c + 1) * 512],
                                                  ps[2][:, 0:512], ft[:, 0:512], ALU.add)
                    with env.group(te, reads=[b_vc, b_w], writes=[b_ps[3]]) as g:
                        g.last = te.matmul(ps[3][:, 0:512], vc_sb[0:64, c * CH:(c + 1) * CH],
                                           we_sb[0:64, :], start=True, stop=True)
                    with env.group(ve, reads=[b_ps[3]], writes=[b_vet]) as g:
                        g.last = ve.tensor_copy(ve_t[:, c * 512:(c + 1) * 512], ps[3][:, 0:512])

            if STAGE >= 6:
                for c in range(NCH):
                    for g_kv in range(HKV):
                        first_chunk = (k == 0 and c == 0)
                        kfcol = g_kv * TT + c * CH

                        def qf_pair(parity):
                            base = xo + 2 * g_kv * TT
                            qp = parity * 64
                            vv = qf[qp:qp + 64, base:base + 2 * TT]
                            return vv.rearrange("p (a t) -> p a t", a=2)[:, :, c * CH:(c + 1) * CH]

                        with env.group(te, reads=[b_kfd, b_qf[slot]], writes=[b_ps[4]]) as g:
                            g.last = te.matmul(ps[4][:, 0:2 * CH],
                                               kf_dup[0:64, kfcol:kfcol + CH],
                                               qf_pair(0), start=True, stop=True)
                        with env.group(te, reads=[b_kfd, b_qf[slot]], writes=[b_ps[5]]) as g:
                            g.last = te.matmul(ps[5][:, 0:2 * CH],
                                               kf_dup[64:128, kfcol:kfcol + CH],
                                               qf_pair(1), start=True, stop=True)
                        with env.group(ve, reads=[b_ps[4], b_consts], writes=[b_am]) as g:
                            g.last = ve.tensor_tensor(am[:, 0:2 * CH], ps[4][:, 0:2 * CH],
                                                      tri_sb[:, 0:2 * CH], ALU.mult)
                        with env.group(ve, reads=[b_ps[5], b_consts], writes=[b_am]) as g:
                            g.last = ve.tensor_tensor(am[:, 2 * CH:4 * CH], ps[5][:, 0:2 * CH],
                                                      tri_sb[:, 0:2 * CH], ALU.mult)
                        with env.group(te, reads=[b_am, b_vet, b_ssb, b_qf[slot]],
                                       writes=[b_ps[6]]) as g:
                            for parity in range(2):
                                op_ = parity * 64
                                g.last = te.matmul(ps[6][op_:op_ + 64, 0:2 * CH],
                                                   ve_t[:, c * 512 + g_kv * 128:c * 512 + g_kv * 128 + 64],
                                                   am[:, parity * 2 * CH:(parity + 1) * 2 * CH],
                                                   start=True, stop=first_chunk)
                                if not first_chunk:
                                    g.last = te.matmul(ps[6][op_:op_ + 64, 0:2 * CH],
                                                       ssb[op_:op_ + 64, g_kv * 66:g_kv * 66 + 64],
                                                       qf_pair(parity), start=False, stop=True)
                        with env.group(te, reads=[b_am, b_ssb, b_qf[slot], b_consts],
                                       writes=[b_ps[3]]) as g:
                            for parity in range(2):
                                dp = parity * 64
                                g.last = te.matmul(ps[3][dp:dp + 1, 0:2 * CH],
                                                   ones_sb[:, :],
                                                   am[:, parity * 2 * CH:(parity + 1) * 2 * CH],
                                                   start=True, stop=first_chunk)
                                if not first_chunk:
                                    g.last = te.matmul(ps[3][dp:dp + 1, 0:2 * CH],
                                                       ssb[dp:dp + 64, g_kv * 66 + 64:g_kv * 66 + 65],
                                                       qf_pair(parity), start=False, stop=True)
                        for parity in range(2):
                            dp = parity * 64
                            with env.group(ve, reads=[b_ps[3]], writes=[b_rrow]) as g:
                                g.last = ve.tensor_scalar(rrow[dp:dp + 1, 0:2 * CH], ps[3][dp:dp + 1, 0:2 * CH],
                                                          1.0, EPS, ALU.mult, ALU.add)
                            with env.group(ve, reads=[b_rrow], writes=[b_rrow]) as g:
                                g.last = ve.reciprocal(rrow[dp:dp + 1, 2 * CH:4 * CH], rrow[dp:dp + 1, 0:2 * CH])
                            with env.group(ve, reads=[b_rrow], writes=[b_rrb]) as g:
                                g.last = ve.tensor_copy(rrb[dp:dp + 1, 0:2 * CH], rrow[dp:dp + 1, 2 * CH:4 * CH])
                        with env.group(te, reads=[b_rrb, b_consts], writes=[b_ps[2]]) as g:
                            te.matmul(ps[2][0:64, 0:2 * CH], onesk1b_sb[0:1, 0:64],
                                      rrb[0:1, 0:2 * CH], start=True, stop=True)
                            g.last = te.matmul(ps[2][64:128, 0:2 * CH], onesk1b_sb[64:65, 0:64],
                                               rrb[64:65, 0:2 * CH], start=True, stop=True)
                        with env.group(ve, reads=[b_ps[2]], writes=[b_rbsb]) as g:
                            g.last = ve.tensor_copy(rb_sb[:, 0:2 * CH], ps[2][:, 0:2 * CH])
                        with env.group(ve, reads=[b_ps[6], b_rbsb], writes=[b_ob[slot]]) as g:
                            for j in range(4):
                                h = 4 * g_kv + j
                                op_ = (j % 2) * 64
                                oc = (j // 2) * CH
                                dcol = xo + (h // 2) * TT + c * CH
                                g.last = ve.tensor_tensor(
                                    ob[op_:op_ + 64, dcol:dcol + CH],
                                    ps[6][op_:op_ + 64, oc:oc + CH],
                                    rb_sb[op_:op_ + 64, oc:oc + CH], ALU.mult)
                        with env.group(te, reads=[b_kft, b_vet], writes=[b_ps[7]]) as g:
                            g.last = te.matmul(ps[7][:, 0:64],
                                               kf_t[:, c * 512 + g_kv * 128:c * 512 + (g_kv + 1) * 128],
                                               ve_t[:, c * 512 + g_kv * 128:c * 512 + g_kv * 128 + 64],
                                               start=True, stop=True)
                        with env.group(te, reads=[b_kft, b_consts], writes=[b_ps[1]]) as g:
                            g.last = te.matmul(ps[1][:, 0:1],
                                               kf_t[:, c * 512 + g_kv * 128:c * 512 + (g_kv + 1) * 128],
                                               ones_sb[:, :], start=True, stop=True)
                        with env.group(ve, reads=[b_ps[7]], writes=[b_ssf]) as g:
                            g.last = ve.tensor_tensor(ssf[:, g_kv * 66:g_kv * 66 + 64],
                                                      ssf[:, g_kv * 66:g_kv * 66 + 64],
                                                      ps[7][:, 0:64], ALU.add)
                        with env.group(ve, reads=[b_ps[1]], writes=[b_ssf]) as g:
                            g.last = ve.tensor_tensor(ssf[:, g_kv * 66 + 64:g_kv * 66 + 65],
                                                      ssf[:, g_kv * 66 + 64:g_kv * 66 + 65],
                                                      ps[1][:, 0:1], ALU.add)
                        with env.group(ve, reads=[b_ssf], writes=[b_ssb]) as g:
                            g.last = ve.tensor_copy(ssb[:, g_kv * 66:g_kv * 66 + 65],
                                                    ssf[:, g_kv * 66:g_kv * 66 + 65])

            if STAGE >= 7:
                for go in range(ND):
                    bk_ = (go % 2)
                    with env.group(te, reads=[b_ob[slot], b_w], writes=[b_ps[bk_]]) as g:
                        for c in range(ND):
                            g.last = te.matmul(ps[bk_][:, 0:TT], wo_sb[:, c * D + go * 128:c * D + (go + 1) * 128],
                                               ob[:, xo + c * TT:xo + (c + 1) * TT],
                                               start=(c == 0), stop=(c == ND - 1))
                    with env.group(ve, reads=[b_ps[bk_], b_consts2], writes=[b_ft]) as g:
                        g.last = ve.tensor_scalar(ft[:, 0:TT], ps[bk_][:, 0:TT],
                                                  bo_sb[:, go:go + 1], None, ALU.add)
                    with env.group(ve, reads=[b_ft, b_ln[slot]], writes=[b_yb]) as g:
                        g.last = ve.tensor_tensor(yb[:, go * TT:(go + 1) * TT], ft[:, 0:TT],
                                                  lnb[:, xo + go * TT:xo + (go + 1) * TT], ALU.add)

                b2 = {"src": b_yb, "xbf": b_xbf, "rowa": b_rowa, "rowa2": b_rowa2,
                      "rstd_b": b_rstdb, "nmr_b": b_nmrb, "dst": b_ob[slot], "consts": b_consts,
                      "consts2": b_consts2,
                      "ps0": b_ps[0], "ps1": b_ps[1], "ps2": b_ps[2], "ps3": b_ps[3]}
                emit_ln_generic(env, te, ve, se, gq, b2, aps,
                                lambda gi: yb[:, gi * TT:(gi + 1) * TT],
                                lambda gi: ob[:, xo + gi * TT:xo + (gi + 1) * TT],
                                gamma=gam_sb, beta=bet_sb)
                with env.group(gq, reads=[b_ob[slot]], writes=[b_xs[slot]]) as g:
                    for gi in range(ND):
                        g.last = gq.tensor_tensor(xs[:, xo + gi * TT:xo + (gi + 1) * TT],
                                                  xs[:, xo + gi * TT:xo + (gi + 1) * TT],
                                                  ob[:, xo + gi * TT:xo + (gi + 1) * TT], ALU.add)
                with qs[slot].group(env, reads=[b_xs[slot]]) as g:
                    for gi in range(ND):
                        g.dma(y[gi * 128:(gi + 1) * 128, t0:t0 + TT],
                              xs[:, xo + gi * TT:xo + (gi + 1) * TT])

            if STAGE < 7:
                with qs[slot].group(env, reads=[b_xs[slot]]) as g:
                    for gi in range(ND):
                        g.dma(y[gi * 128:(gi + 1) * 128, t0:t0 + TT],
                              xs[:, xo + gi * TT:xo + (gi + 1) * TT])

        env.wait_tok(env.sync, (p.sem("qs0"), p.sem("qs0").n))
        env.wait_tok(env.sync, (p.sem("qs1"), p.sem("qs1").n))

    counts = p.run(orchestrate)
    return nc, counts


def host_inputs_l0attn(xT, Wq, bq, Wk, bk, Wv, bv, Wc, We, Wo, bo, gamma, beta):
    import ml_dtypes
    tri = (np.arange(128)[:, None] <= np.arange(CH)[None, :]).astype(np.float32)
    trimask = np.tile(tri, (1, 4)).astype(ml_dtypes.bfloat16)
    We = np.asarray(We, np.float32)
    wedup = np.concatenate([np.concatenate([We[:, 64 * g:64 * (g + 1)]] * 2, axis=1)
                            for g in range(4)], axis=1)
    rs = lambda v: np.ascontiguousarray(np.asarray(v, np.float32).reshape(-1, 128).T)
    return {
        "x": np.ascontiguousarray(xT, np.float32),
        "Wq": np.ascontiguousarray(Wq, np.float32),
        "Wk": np.ascontiguousarray(Wk, np.float32),
        "Wv": np.ascontiguousarray(Wv, np.float32),
        "Wc": np.ascontiguousarray(Wc, np.float32),
        "WeDup": np.ascontiguousarray(wedup),
        "Wo": np.ascontiguousarray(Wo, np.float32),
        "bq": rs(bq), "bk": rs(bk), "bv": rs(bv), "bo": rs(bo),
        "gamma": rs(gamma), "beta": rs(beta),
        "ones128": np.ones((128, 1), ml_dtypes.bfloat16),
        "onesk1f": np.ones((1, 128), np.float32),
        "negk1f": np.full((1, 128), -1.0, np.float32),
        "onesk1b": np.ones((128, 128), ml_dtypes.bfloat16),
        "trimask": trimask,
    }


"""L0 ffn phase: y = x + ffn(layernorm(x)), RWKV channel mixer. Feature-major.
Race-detector-clean: dependent ops are in separate groups (explicit sem edges)."""

TT = 512
NTT = T // TT


def emit_ln(env, te, ve, se, gq, b, aps, ln_dst):
    xs, xo, xbf = aps["xs"], aps["xo"], aps["xbf"]
    rowa, rstd_b, nmr_b = aps["rowa"], aps["rstd_b"], aps["nmr_b"]
    ones_sb, onesk1_sb, negk1_sb = aps["ones_sb"], aps["onesk1_sb"], aps["negk1_sb"]
    ps = aps["ps"]
    eps = aps.get("eps", 1e-5)
    b_xs = b["xs"]

    with env.group(ve, reads=[b_xs], writes=[b["xbf"]]) as g:
        for gi in range(ND):
            g.last = ve.tensor_copy(xbf[:, gi * TT:(gi + 1) * TT], xs[:, xo + gi * TT:xo + (gi + 1) * TT])
    with env.group(te, reads=[b["xbf"], b["consts"]], writes=[b["ps0"]]) as g:
        for gi in range(ND):
            g.last = te.matmul(ps[0][0:1, 0:TT], ones_sb[:, :], xbf[:, gi * TT:(gi + 1) * TT],
                               start=(gi == 0), stop=(gi == ND - 1))
    with env.group(ve, writes=[b["xbf"]]) as g:
        for gi in range(ND):
            g.last = ve.tensor_mul(xbf[:, gi * TT:(gi + 1) * TT], xbf[:, gi * TT:(gi + 1) * TT],
                                   xbf[:, gi * TT:(gi + 1) * TT])
    with env.group(te, reads=[b["xbf"], b["consts"]], writes=[b["ps1"]]) as g:
        for gi in range(ND):
            g.last = te.matmul(ps[1][0:1, 0:TT], ones_sb[:, :], xbf[:, gi * TT:(gi + 1) * TT],
                               start=(gi == 0), stop=(gi == ND - 1))
    with env.group(ve, reads=[b["ps0"], b["ps1"]], writes=[b["rowa"]]) as g:
        ve.tensor_scalar(rowa[0:1, 0:TT], ps[0][0:1, 0:TT], 1.0 / D, None, ALU.mult)
        g.last = ve.tensor_scalar(rowa[0:1, TT:2 * TT], ps[1][0:1, 0:TT], 1.0 / D, eps, ALU.mult, ALU.add)
    with env.group(ve, writes=[b["rowa2"]], reads=[b["rowa"]]) as g:
        g.last = ve.tensor_mul(rowa[0:1, 2 * TT:3 * TT], rowa[0:1, 0:TT], rowa[0:1, 0:TT])
    with env.group(ve, reads=[b["rowa2"]], writes=[b["rowa"]]) as g:
        g.last = ve.tensor_tensor(rowa[0:1, TT:2 * TT], rowa[0:1, TT:2 * TT],
                                  rowa[0:1, 2 * TT:3 * TT], ALU.subtract)
    with env.group(se, reads=[b["rowa"]], writes=[b["rowa2"]]) as g:
        g.last = se.activation(rowa[0:1, 2 * TT:3 * TT], rowa[0:1, TT:2 * TT], AF.Sqrt)
    with env.group(ve, reads=[b["rowa2"]], writes=[b["rowa"]]) as g:
        g.last = ve.reciprocal(rowa[0:1, TT:2 * TT], rowa[0:1, 2 * TT:3 * TT])
    with env.group(ve, reads=[b["rowa"]], writes=[b["rowa2"]]) as g:
        g.last = ve.tensor_mul(rowa[0:1, 2 * TT:3 * TT], rowa[0:1, 0:TT], rowa[0:1, TT:2 * TT])
    with env.group(te, reads=[b["rowa"], b["rowa2"], b["consts"]], writes=[b["ps2"], b["ps3"]]) as g:
        te.matmul(ps[2][:, 0:TT], onesk1_sb[0:1, :], rowa[0:1, TT:2 * TT], start=True, stop=True)
        g.last = te.matmul(ps[3][:, 0:TT], negk1_sb[0:1, :], rowa[0:1, 2 * TT:3 * TT], start=True, stop=True)
    with env.group(ve, reads=[b["ps2"], b["ps3"]], writes=[b["rstd_b"], b["nmr_b"]]) as g:
        ve.tensor_copy(rstd_b[:, :], ps[2][:, 0:TT])
        g.last = ve.tensor_copy(nmr_b[:, :], ps[3][:, 0:TT])
    with env.group(ve, reads=[b_xs, b["rstd_b"]], writes=[b["ln"]]) as g:
        for gi in range(ND):
            g.last = ve.tensor_tensor(ln_dst(gi), xs[:, xo + gi * TT:xo + (gi + 1) * TT],
                                      rstd_b[:, :], ALU.mult)
    with env.group(gq, reads=[b["nmr_b"]], writes=[b["ln"]]) as g:
        for gi in range(ND):
            g.last = gq.tensor_tensor(ln_dst(gi), ln_dst(gi), nmr_b[:, :], ALU.add)


def build_l0ffn():
    nc = bass.Bass("TRN2", target_bir_lowering=False)
    x = nc.dram_tensor("x", [D, T], F32, kind="ExternalInput").ap()
    WrT = nc.dram_tensor("WrT", [D, D], F32, kind="ExternalInput").ap()
    WkT = nc.dram_tensor("WkT", [D, D], F32, kind="ExternalInput").ap()
    WvT = nc.dram_tensor("WvT", [D, D], F32, kind="ExternalInput").ap()
    mur = nc.dram_tensor("mur", [128, ND], F32, kind="ExternalInput").ap()
    muk = nc.dram_tensor("muk", [128, ND], F32, kind="ExternalInput").ap()
    ones128 = nc.dram_tensor("ones128", [128, 1], BF16, kind="ExternalInput").ap()
    onesk1f = nc.dram_tensor("onesk1f", [1, 128], F32, kind="ExternalInput").ap()
    negk1f = nc.dram_tensor("negk1f", [1, 128], F32, kind="ExternalInput").ap()
    y = nc.dram_tensor("y", [D, T], F32, kind="ExternalOutput").ap()

    TP = T + 1
    ln = halloc(nc, "ln", [128, ND * TP], BF16)
    wfull = {k: halloc(nc, f"w{k}_full", [128, ND * D], BF16) for k in "rkv"}
    mur_sb = halloc(nc, "mur_sb", [128, ND], F32)
    muk_sb = halloc(nc, "muk_sb", [128, ND], F32)
    mur1_sb = halloc(nc, "mur1_sb", [128, ND], F32)
    muk1_sb = halloc(nc, "muk1_sb", [128, ND], F32)
    ones_sb = halloc(nc, "ones_sb", [128, 1], BF16)
    onesk1_sb = halloc(nc, "onesk1_sb", [1, 128], F32)
    negk1_sb = halloc(nc, "negk1_sb", [1, 128], F32)

    NS = 2
    xs = halloc(nc, "xs", [128, NS * ND * TT], F32)
    xbf = halloc(nc, "xbf", [128, ND * TT], BF16)
    rowa = halloc(nc, "rowa", [1, 3 * TT], F32)
    rstd_b = halloc(nc, "rstd_b", [128, TT], F32)
    nmr_b = halloc(nc, "nmr_b", [128, TT], F32)
    xr = halloc(nc, "xr", [128, ND * TT], BF16)
    xk = halloc(nc, "xk", [128, ND * TT], BF16)
    rbuf = halloc(nc, "rbuf", [128, ND * TT], BF16)
    krelu = halloc(nc, "krelu", [128, 2 * TT], BF16)
    kact = halloc(nc, "kact", [128, ND * TT], BF16)
    t1 = halloc(nc, "t1", [128, 2 * TT], F32)

    ps = [palloc(nc, f"ps{i}", 512) for i in range(8)]

    p = Prog(nc)

    def orchestrate(env):
        gq, te, ve, se = env.gpsimd, env.tensor, env.vector, env.scalar
        qw = DQ(p, "gpsimd", "qw")
        qx = [DQ(p, "sync", f"qx{i}") for i in range(NS)]
        qs = [DQ(p, "sync", f"qs{i}") for i in range(NS)]
        qc = DQ(p, "sync", "qc")
        B = p.buf
        b_w = B("w"); b_mu = B("mu"); b_mu1 = B("mu1"); b_ln = B("ln")
        b_consts = B("consts")
        b_xs = [B(f"xs{i}") for i in range(NS)]
        b_xbf = B("xbf"); b_rowa = B("rowa"); b_rowa2 = B("rowa2")
        b_rstdb = B("rstdb"); b_nmrb = B("nmrb")
        b_xr = B("xr"); b_xk = B("xk"); b_rbuf = B("rbuf")
        b_kr = [B("kr0"), B("kr1")]; b_kact = B("kact")
        b_t1 = [B("t1a"), B("t1b")]
        b_ps = [B(f"ps{i}") for i in range(8)]

        with qw.group(env, writes=[b_w]) as g:
            for key, src in (("r", WrT), ("k", WkT), ("v", WvT)):
                for c in range(ND):
                    g.dma(wfull[key][:, c * D:(c + 1) * D], src[c * 128:(c + 1) * 128, :])
        with qc.group(env, writes=[b_mu, b_consts]) as g:
            g.dma(mur_sb[:, :], mur[:, :])
            g.dma(muk_sb[:, :], muk[:, :])
            g.dma(onesk1_sb[:, :], onesk1f[:, :])
            g.dma(negk1_sb[:, :], negk1f[:, :])
            g.dma(ones_sb[:, :], ones128[:, :])

        with env.group(ve, reads=[b_mu], writes=[b_mu1]) as g:
            ve.tensor_scalar(mur1_sb[:, :], mur_sb[:, :], -1.0, 1.0, ALU.mult, ALU.add)
            g.last = ve.tensor_scalar(muk1_sb[:, :], muk_sb[:, :], -1.0, 1.0, ALU.mult, ALU.add)
        with env.group(ve, writes=[b_ln]) as g:
            for gi in range(ND):
                g.last = ve.memset(ln[:, gi * TP:gi * TP + 1], 0.0)

        for k in range(NTT):
            slot = k % NS
            xo = slot * ND * TT
            t0 = k * TT

            with qx[slot].group(env, writes=[b_xs[slot]]) as g:
                for gi in range(ND):
                    g.dma(xs[:, xo + gi * TT:xo + (gi + 1) * TT],
                          x[gi * 128:(gi + 1) * 128, t0:t0 + TT])

            b = {"xs": b_xs[slot], "xbf": b_xbf, "rowa": b_rowa, "rowa2": b_rowa2,
                 "rstd_b": b_rstdb, "nmr_b": b_nmrb, "ln": b_ln, "consts": b_consts,
                 "ps0": b_ps[0], "ps1": b_ps[1], "ps2": b_ps[2], "ps3": b_ps[3]}
            aps = {"xs": xs, "xo": xo, "xbf": xbf, "rowa": rowa, "rstd_b": rstd_b,
                   "nmr_b": nmr_b, "ones_sb": ones_sb, "onesk1_sb": onesk1_sb,
                   "negk1_sb": negk1_sb, "ps": ps}
            emit_ln(env, te, ve, se, gq, b, aps,
                    lambda gi: ln[:, gi * TP + 1 + t0:gi * TP + 1 + t0 + TT])

            with env.group(ve, reads=[b_ln, b_mu], writes=[b_xr, b_xk]) as g:
                for gi in range(ND):
                    cur = ln[:, gi * TP + 1 + t0:gi * TP + 1 + t0 + TT]
                    ve.tensor_scalar(xr[:, gi * TT:(gi + 1) * TT], cur, mur_sb[:, gi:gi + 1], None, ALU.mult)
                    g.last = ve.tensor_scalar(xk[:, gi * TT:(gi + 1) * TT], cur, muk_sb[:, gi:gi + 1], None, ALU.mult)
            with env.group(ve, reads=[b_ln, b_mu1], writes=[b_xr, b_xk]) as g:
                for gi in range(ND):
                    prev = ln[:, gi * TP + t0:gi * TP + t0 + TT]
                    ve.scalar_tensor_tensor(xr[:, gi * TT:(gi + 1) * TT], prev, mur1_sb[:, gi:gi + 1],
                                            xr[:, gi * TT:(gi + 1) * TT], ALU.mult, ALU.add)
                    g.last = ve.scalar_tensor_tensor(xk[:, gi * TT:(gi + 1) * TT], prev, muk1_sb[:, gi:gi + 1],
                                                     xk[:, gi * TT:(gi + 1) * TT], ALU.mult, ALU.add)

            for go in range(ND):
                bk = 4 + (go % 2)
                with env.group(te, reads=[b_xr, b_w], writes=[b_ps[bk]]) as g:
                    for c in range(ND):
                        g.last = te.matmul(ps[bk][:, 0:TT], wfull["r"][:, c * D + go * 128:c * D + (go + 1) * 128],
                                           xr[:, c * TT:(c + 1) * TT], start=(c == 0), stop=(c == ND - 1))
                with env.group(se, reads=[b_ps[bk]], writes=[b_rbuf]) as g:
                    g.last = se.activation(rbuf[:, go * TT:(go + 1) * TT], ps[bk][:, 0:TT], AF.Sigmoid)

            for go in range(ND):
                bk = 6 + (go % 2)
                with env.group(te, reads=[b_xk, b_w], writes=[b_ps[bk]]) as g:
                    for c in range(ND):
                        g.last = te.matmul(ps[bk][:, 0:TT], wfull["k"][:, c * D + go * 128:c * D + (go + 1) * 128],
                                           xk[:, c * TT:(c + 1) * TT], start=(c == 0), stop=(c == ND - 1))
                with env.group(se, reads=[b_ps[bk]], writes=[b_kr[go % 2]]) as g:
                    g.last = se.activation(krelu[:, (go % 2) * TT:(go % 2 + 1) * TT], ps[bk][:, 0:TT], AF.Relu)
                with env.group(ve, reads=[b_ps[bk], b_kr[go % 2]], writes=[b_kact]) as g:
                    g.last = ve.tensor_tensor(kact[:, go * TT:(go + 1) * TT], ps[bk][:, 0:TT],
                                              krelu[:, (go % 2) * TT:(go % 2 + 1) * TT], ALU.mult)

            for go in range(ND):
                bk = 4 + (go % 2)
                with env.group(te, reads=[b_kact, b_w], writes=[b_ps[bk]]) as g:
                    for c in range(ND):
                        g.last = te.matmul(ps[bk][:, 0:TT], wfull["v"][:, c * D + go * 128:c * D + (go + 1) * 128],
                                           kact[:, c * TT:(c + 1) * TT], start=(c == 0), stop=(c == ND - 1))
                with env.group(ve, reads=[b_ps[bk], b_rbuf], writes=[b_t1[go % 2]]) as g:
                    g.last = ve.tensor_tensor(t1[:, (go % 2) * TT:(go % 2 + 1) * TT], ps[bk][:, 0:TT],
                                              rbuf[:, go * TT:(go + 1) * TT], ALU.mult)
                with env.group(gq, reads=[b_t1[go % 2]], writes=[b_xs[slot]]) as g:
                    g.last = gq.tensor_tensor(xs[:, xo + go * TT:xo + (go + 1) * TT],
                                              xs[:, xo + go * TT:xo + (go + 1) * TT],
                                              t1[:, (go % 2) * TT:(go % 2 + 1) * TT], ALU.add)

            with qs[slot].group(env, reads=[b_xs[slot]]) as g:
                for gi in range(ND):
                    g.dma(y[gi * 128:(gi + 1) * 128, t0:t0 + TT],
                          xs[:, xo + gi * TT:xo + (gi + 1) * TT])

        env.wait_tok(env.sync, (p.sem("qs0"), p.sem("qs0").n))
        env.wait_tok(env.sync, (p.sem("qs1"), p.sem("qs1").n))

    counts = p.run(orchestrate)
    return nc, counts


def host_inputs_l0ffn(xT, Wr, Wk, Wv, mu_r, mu_k):
    import ml_dtypes
    return {
        "x": np.ascontiguousarray(xT, np.float32),
        "WrT": np.ascontiguousarray(np.asarray(Wr).T, np.float32),
        "WkT": np.ascontiguousarray(np.asarray(Wk).T, np.float32),
        "WvT": np.ascontiguousarray(np.asarray(Wv).T, np.float32),
        "mur": np.ascontiguousarray(np.asarray(mu_r, np.float32).reshape(ND, 128).T),
        "muk": np.ascontiguousarray(np.asarray(mu_k, np.float32).reshape(ND, 128).T),
        "ones128": np.ones((128, 1), ml_dtypes.bfloat16),
        "onesk1f": np.ones((1, 128), np.float32),
        "negk1f": np.full((1, 128), -1.0, np.float32),
    }


"""L1 ffn (MoE) phase: y = x + moe(layernorm(x)).
Dense all-8-experts + shared path; top-2 gates; HBM fp32 accumulator via DMA accum-add.
Feature-major. Path order: shared first (initializes accumulator), then experts."""

H4 = 4096
NH = H4 // 128
NHH = NH // 2
NEX = 8
NCHUNK = TT // 128


def build_moe():
    nc = bass.Bass("TRN2", target_bir_lowering=False)
    x = nc.dram_tensor("x", [D, T], F32, kind="ExternalInput").ap()
    sw1T = nc.dram_tensor("sw1T", [D, H4], F32, kind="ExternalInput").ap()
    sw2T = nc.dram_tensor("sw2T", [H4, D], F32, kind="ExternalInput").ap()
    ew1T = nc.dram_tensor("ew1T", [NEX, D, H4], F32, kind="ExternalInput").ap()
    ew2T = nc.dram_tensor("ew2T", [NEX, H4, D], F32, kind="ExternalInput").ap()
    gwT = nc.dram_tensor("gwT", [D, NEX], F32, kind="ExternalInput").ap()
    ones128 = nc.dram_tensor("ones128", [128, 1], BF16, kind="ExternalInput").ap()
    onesk1f = nc.dram_tensor("onesk1f", [1, 128], F32, kind="ExternalInput").ap()
    negk1f = nc.dram_tensor("negk1f", [1, 128], F32, kind="ExternalInput").ap()
    eye128f = nc.dram_tensor("eye128f", [128, 128], F32, kind="ExternalInput").ap()
    onesk1b = nc.dram_tensor("onesk1b", [1, 128], BF16, kind="ExternalInput").ap()
    y = nc.dram_tensor("y", [D, T], F32, kind="ExternalOutput").ap()
    acc = nc.dram_tensor("acc_scr", [D, T], F32).ap()
    gtmp = nc.dram_tensor("gtmp_scr", [NEX, T], BF16).ap()

    ln = halloc(nc, "ln", [128, ND * T], BF16)
    w1h = halloc(nc, "w1h", [128, ND * (H4 // 2)], BF16)
    w2h = halloc(nc, "w2h", [128, NHH * D], BF16)
    sh = halloc(nc, "sh", [128, NHH * TT], BF16)
    NS = 2
    xs = halloc(nc, "xs", [128, NS * ND * TT], F32)
    xbf = sh
    ot = halloc(nc, "ot", [128, 4 * TT], F32)
    rowa = halloc(nc, "rowa", [1, 3 * TT], F32)
    rstd_b = halloc(nc, "rstd_b", [128, TT], F32)
    nmr_b = halloc(nc, "nmr_b", [128, TT], F32)
    ones_sb = halloc(nc, "ones_sb", [128, 1], BF16)
    onesk1_sb = halloc(nc, "onesk1_sb", [1, 128], F32)
    negk1_sb = halloc(nc, "negk1_sb", [1, 128], F32)
    eye_sb = halloc(nc, "eye_sb", [128, 128], F32)
    onesk1b_sb = halloc(nc, "onesk1b_sb", [1, 128], BF16)
    gw_f = halloc(nc, "gw_f", [128, ND * NEX], F32)
    gw_hi = halloc(nc, "gw_hi", [128, ND * NEX], BF16)
    gw_lo = halloc(nc, "gw_lo", [128, ND * NEX], BF16)
    lg = halloc(nc, "lg", [8, TT], F32)
    ltok = halloc(nc, "ltok", [128, NCHUNK * NEX], F32)
    gtok = halloc(nc, "gtok", [128, NCHUNK * NEX], F32)
    tops = halloc(nc, "tops", [128, 8], F32)
    ohot = halloc(nc, "ohot", [128, 2 * NEX], F32)
    gsb = halloc(nc, "gsb", [8, TT], BF16)
    grow = halloc(nc, "grow", [1, 2 * TT], BF16)
    gb = halloc(nc, "gb", [128, 2 * TT], BF16)

    ps = [palloc(nc, f"ps{i}", 512) for i in range(8)]

    p = Prog(nc)

    def orchestrate(env):
        gq, te, ve, se = env.gpsimd, env.tensor, env.vector, env.scalar
        qw = DQ(p, "gpsimd", "qw")
        qa = DQ(p, "gpsimd", "qa")
        qx = [DQ(p, "sync", f"qx{i}") for i in range(NS)]
        qs = [DQ(p, "sync", f"qs{i}") for i in range(NS)]
        qc = DQ(p, "sync", "qc")
        qg = [DQ(p, "sync", f"qg{i}") for i in range(2)]
        qgs = DQ(p, "sync", "qgs")
        qacl = [DQ(p, "sync", f"qacl{i}") for i in range(NS)]
        B = p.buf
        b_w = B("w")
        b_consts = B("consts")
        b_gw = B("gw"); b_gwhl = B("gwhl")
        b_ln = B("ln")
        b_xs = [B(f"xs{i}") for i in range(NS)]
        b_rowa = B("rowa"); b_rowa2 = B("rowa2")
        b_rstdb = B("rstdb"); b_nmrb = B("nmrb")
        b_lg = B("lg"); b_ltok = B("ltok"); b_gtok = B("gtok")
        b_tops = B("tops"); b_ohot = B("ohot"); b_gsb = B("gsb")
        b_grow = [B("grow0"), B("grow1")]
        b_gb = [B("gb0"), B("gb1")]
        b_sh = B("sh"); b_gtmp = B("gtmp")
        b_ot = B("ot")
        b_acc = B("acc")
        b_ps = [B(f"ps{i}") for i in range(8)]

        with qc.group(env, writes=[b_consts]) as g:
            g.dma(ones_sb[:, :], ones128[:, :])
            g.dma(onesk1_sb[:, :], onesk1f[:, :])
            g.dma(negk1_sb[:, :], negk1f[:, :])
            g.dma(eye_sb[:, :], eye128f[:, :])
            g.dma(onesk1b_sb[:, :], onesk1b[:, :])
        with qc.group(env, writes=[b_gw], extra=[b_consts.w]) as g:
            for c in range(ND):
                g.dma(gw_f[:, c * NEX:(c + 1) * NEX], gwT[c * 128:(c + 1) * 128, :])
        with env.group(ve, reads=[b_gw], writes=[b_gwhl]) as g:
            g.last = ve.tensor_copy(gw_hi[:, :], gw_f[:, :])
        with env.group(ve, reads=[b_gw], writes=[b_gwhl]) as g:
            g.last = ve.tensor_tensor(gw_lo[:, :], gw_f[:, :], gw_hi[:, :], ALU.subtract)

        for k in range(NTT):
            slot = k % NS
            xo = slot * ND * TT
            t0 = k * TT
            with qx[slot].group(env, writes=[b_xs[slot]]) as g:
                for gi in range(ND):
                    g.dma(xs[:, xo + gi * TT:xo + (gi + 1) * TT],
                          x[gi * 128:(gi + 1) * 128, t0:t0 + TT])
            b = {"xs": b_xs[slot], "xbf": b_sh, "rowa": b_rowa, "rowa2": b_rowa2,
                 "rstd_b": b_rstdb, "nmr_b": b_nmrb, "ln": b_ln, "consts": b_consts,
                 "ps0": b_ps[0], "ps1": b_ps[1], "ps2": b_ps[2], "ps3": b_ps[3]}
            aps = {"xs": xs, "xo": xo, "xbf": xbf, "rowa": rowa, "rstd_b": rstd_b,
                   "nmr_b": nmr_b, "ones_sb": ones_sb, "onesk1_sb": onesk1_sb,
                   "negk1_sb": negk1_sb, "ps": ps}
            emit_ln(env, te, ve, se, gq, b, aps,
                    lambda gi, t0=t0: ln[:, gi * T + t0:gi * T + t0 + TT])

            with env.group(te, reads=[b_ln, b_gwhl], writes=[b_ps[4]]) as g:
                for c in range(ND):
                    te.matmul(ps[4][0:8, 0:TT], gw_hi[:, c * NEX:(c + 1) * NEX],
                              ln[:, c * T + t0:c * T + t0 + TT], start=(c == 0), stop=False)
                for c in range(ND):
                    g.last = te.matmul(ps[4][0:8, 0:TT], gw_lo[:, c * NEX:(c + 1) * NEX],
                                       ln[:, c * T + t0:c * T + t0 + TT],
                                       start=False, stop=(c == ND - 1))
            with env.group(ve, reads=[b_ps[4]], writes=[b_lg]) as g:
                g.last = ve.tensor_copy(lg[0:8, 0:TT], ps[4][0:8, 0:TT])
            with env.group(te, reads=[b_lg, b_consts], writes=[b_ps[5]]) as g:
                for c in range(NCHUNK):
                    g.last = te.transpose(ps[5][:, c * NEX:(c + 1) * NEX],
                                          lg[0:8, c * 128:(c + 1) * 128], eye_sb[0:8, 0:8])
            with env.group(ve, reads=[b_ps[5]], writes=[b_ltok]) as g:
                g.last = ve.tensor_copy(ltok[:, :], ps[5][:, 0:NCHUNK * NEX])
            for c in range(NCHUNK):
                lt = ltok[:, c * NEX:(c + 1) * NEX]
                with env.group(ve, reads=[b_ltok], writes=[b_tops]) as g:
                    g.last = ve.reduce_max(tops[:, 0:1], lt, AX.X)
                with env.group(ve, reads=[b_ltok, b_tops], writes=[b_ohot]) as g:
                    g.last = ve.tensor_scalar(ohot[:, 0:NEX], lt, tops[:, 0:1], None, ALU.is_equal)
                with env.group(ve, reads=[b_ltok, b_ohot], writes=[b_ohot]) as g:
                    g.last = ve.scalar_tensor_tensor(ohot[:, NEX:2 * NEX], ohot[:, 0:NEX], -1e30,
                                                     lt, ALU.mult, ALU.add)
                with env.group(ve, reads=[b_ohot], writes=[b_tops]) as g:
                    g.last = ve.reduce_max(tops[:, 1:2], ohot[:, NEX:2 * NEX], AX.X)
                with env.group(ve, reads=[b_tops], writes=[b_tops]) as g:
                    g.last = ve.tensor_tensor(tops[:, 4:5], tops[:, 0:1], tops[:, 1:2], ALU.subtract)
                with env.group(se, reads=[b_tops], writes=[b_tops]) as g:
                    g.last = se.activation(tops[:, 2:3], tops[:, 4:5], AF.Sigmoid)
                with env.group(ve, reads=[b_tops], writes=[b_tops]) as g:
                    g.last = ve.tensor_scalar(tops[:, 3:4], tops[:, 2:3], -1.0, 1.0, ALU.mult, ALU.add)
                with env.group(ve, reads=[b_tops], writes=[b_ohot]) as g:
                    g.last = ve.tensor_scalar(ohot[:, NEX:2 * NEX], ohot[:, NEX:2 * NEX],
                                              tops[:, 1:2], None, ALU.is_equal)
                with env.group(ve, reads=[b_ohot, b_tops], writes=[b_gtok]) as g:
                    g.last = ve.tensor_scalar(gtok[:, c * NEX:(c + 1) * NEX], ohot[:, 0:NEX],
                                              tops[:, 2:3], None, ALU.mult)
                with env.group(ve, reads=[b_ohot, b_tops], writes=[b_gtok]) as g:
                    g.last = ve.scalar_tensor_tensor(gtok[:, c * NEX:(c + 1) * NEX], ohot[:, NEX:2 * NEX],
                                                     tops[:, 3:4], gtok[:, c * NEX:(c + 1) * NEX],
                                                     ALU.mult, ALU.add)
            with env.group(te, reads=[b_gtok, b_consts], writes=[b_ps[6]]) as g:
                for c in range(NCHUNK):
                    g.last = te.transpose(ps[6][0:8, c * 128:(c + 1) * 128],
                                          gtok[:, c * NEX:(c + 1) * NEX], eye_sb[:, :])
            with env.group(ve, reads=[b_ps[6]], writes=[b_gsb]) as g:
                g.last = ve.tensor_copy(gsb[0:8, 0:TT], ps[6][0:8, 0:TT])
            with qgs.group(env, reads=[b_gsb], writes=[b_gtmp]) as g:
                g.dma(gtmp[:, t0:t0 + TT], gsb[0:8, 0:TT])

        paths = [None] + list(range(NEX))
        for pi, e in enumerate(paths):
            shared = e is None
            for hh in range(2):
                h0 = hh * (H4 // 2)
                with qw.group(env, writes=[b_w]) as g:
                    src1 = sw1T if shared else ew1T[e]
                    src2 = sw2T if shared else ew2T[e]
                    for c in range(ND):
                        g.dma(w1h[:, c * (H4 // 2):(c + 1) * (H4 // 2)],
                              src1[c * 128:(c + 1) * 128, h0:h0 + H4 // 2])
                    for htt in range(NHH):
                        g.dma(w2h[:, htt * D:(htt + 1) * D],
                              src2[h0 + htt * 128:h0 + (htt + 1) * 128, :])
                for k in range(NTT):
                    t0 = k * TT
                    gslot = k % 2
                    if not shared:
                        with qg[gslot].group(env, reads=[b_gtmp], writes=[b_grow[gslot]]) as g:
                            g.dma(grow[0:1, gslot * TT:(gslot + 1) * TT], gtmp[e:e + 1, t0:t0 + TT])
                        with env.group(te, reads=[b_grow[gslot], b_consts], writes=[b_ps[2]]) as g:
                            g.last = te.matmul(ps[2][:, 0:TT], onesk1b_sb[0:1, :],
                                               grow[0:1, gslot * TT:(gslot + 1) * TT], start=True, stop=True)
                        with env.group(ve, reads=[b_ps[2]], writes=[b_gb[gslot]]) as g:
                            g.last = ve.tensor_copy(gb[:, gslot * TT:(gslot + 1) * TT], ps[2][:, 0:TT])
                    for ht in range(NHH):
                        bk = 4 + (ht % 2)
                        with env.group(te, reads=[b_ln, b_w], writes=[b_ps[bk]]) as g:
                            for c in range(ND):
                                g.last = te.matmul(ps[bk][:, 0:TT],
                                                   w1h[:, c * (H4 // 2) + ht * 128:c * (H4 // 2) + (ht + 1) * 128],
                                                   ln[:, c * T + t0:c * T + t0 + TT],
                                                   start=(c == 0), stop=(c == ND - 1))
                        with env.group(se, reads=[b_ps[bk]], writes=[b_sh]) as g:
                            g.last = se.activation(sh[:, ht * TT:(ht + 1) * TT], ps[bk][:, 0:TT], AF.Silu)
                        if not shared:
                            with env.group(ve, reads=[b_gb[gslot]], writes=[b_sh]) as g:
                                g.last = ve.tensor_tensor(sh[:, ht * TT:(ht + 1) * TT],
                                                          sh[:, ht * TT:(ht + 1) * TT],
                                                          gb[:, gslot * TT:(gslot + 1) * TT], ALU.mult)
                    for dhalf in range(2):
                        for gi in range(4):
                            go = dhalf * 4 + gi
                            bk = 6 + (gi % 2)
                            with env.group(te, reads=[b_sh, b_w], writes=[b_ps[bk]]) as g:
                                for ht in range(NHH):
                                    g.last = te.matmul(ps[bk][:, 0:TT],
                                                       w2h[:, ht * D + go * 128:ht * D + (go + 1) * 128],
                                                       sh[:, ht * TT:(ht + 1) * TT],
                                                       start=(ht == 0), stop=(ht == NHH - 1))
                            with env.group(ve, reads=[b_ps[bk]], writes=[b_ot]) as g:
                                g.last = ve.tensor_copy(ot[:, gi * TT:(gi + 1) * TT], ps[bk][:, 0:TT])
                        first = shared and hh == 0
                        with qa.group(env, reads=[b_ot], writes=[b_acc]) as g:
                            for gi in range(4):
                                go = dhalf * 4 + gi
                                g.dma(acc[go * 128:(go + 1) * 128, t0:t0 + TT],
                                      ot[:, gi * TT:(gi + 1) * TT],
                                      **({} if first else {"accum_op": ALU.add}))

        for k in range(NTT):
            t0 = k * TT
            with qx[0].group(env, writes=[b_xs[0]]) as g:
                for gi in range(ND):
                    g.dma(xs[:, gi * TT:(gi + 1) * TT],
                          x[gi * 128:(gi + 1) * 128, t0:t0 + TT])
            with qacl[0].group(env, writes=[b_xs[1]],
                               extra=[b_acc.w]) as g:
                for gi in range(ND):
                    g.dma(xs[:, ND * TT + gi * TT:ND * TT + (gi + 1) * TT],
                          acc[gi * 128:(gi + 1) * 128, t0:t0 + TT])
            with env.group(gq, reads=[b_xs[1]], writes=[b_xs[0]]) as g:
                for gi in range(ND):
                    g.last = gq.tensor_tensor(xs[:, gi * TT:(gi + 1) * TT],
                                              xs[:, gi * TT:(gi + 1) * TT],
                                              xs[:, ND * TT + gi * TT:ND * TT + (gi + 1) * TT], ALU.add)
            with qs[0].group(env, reads=[b_xs[0]]) as g:
                for gi in range(ND):
                    g.dma(y[gi * 128:(gi + 1) * 128, t0:t0 + TT],
                          xs[:, gi * TT:(gi + 1) * TT])

        env.wait_tok(env.sync, (p.sem("qs0"), p.sem("qs0").n))

    counts = p.run(orchestrate)
    return nc, counts


"""L1 attn phase: y = x + (qs * v) @ Wc.T ; qs = associative-scan of [ident; q],
combine(l, r) = rms(l @ P1.T + r @ P2.T); P = polar(Wp) via fp16 Newton-Schulz.
Feature-major. PSUM rule: never two unevacuated writes to different col-ranges
of one bank. SBUF pooled via bitcast between NS and scan phases."""

TT = 512
NTT = T // TT
RMS_EPS = 1e-8

NS_SCHED = [(4.0, -10.375, 6.820069), (4.15, -11.5, 8.080346), (4.15, -11.5, 8.080346),
            (4.15, -11.5, 8.080346), (4.15, -11.5, 8.080346), (4.15, -11.5, 8.080346),
            (4.15, -11.5, 8.080346), (4.1, -11.125, 7.797593), (3.95, -10.0, 6.833978),
            (3.4, -6.625, 4.131601)]
NS_CUBIC = 4
POWER_ITERS = 25
PRESCALE = 1.25

LEVELS = []
_m = T + 1
while True:
    LEVELS.append(_m)
    if _m == 1:
        break
    _m //= 2
NLEV = len(LEVELS)
PITCH = 1026


def build_l1attn(ns_only=False):
    nc = bass.Bass("TRN2", target_bir_lowering=False)
    x = nc.dram_tensor("x", [D, T], F32, kind="ExternalInput").ap()
    WqT = nc.dram_tensor("WqT", [D, D], F32, kind="ExternalInput").ap()
    WvT = nc.dram_tensor("WvT", [D, D], F32, kind="ExternalInput").ap()
    WcT = nc.dram_tensor("WcT", [D, D], F32, kind="ExternalInput").ap()
    Wp = [nc.dram_tensor("Wp1", [D, D], F32, kind="ExternalInput").ap(),
          nc.dram_tensor("Wp2", [D, D], F32, kind="ExternalInput").ap()]
    WpT = [nc.dram_tensor("Wp1T", [D, D], F32, kind="ExternalInput").ap(),
           nc.dram_tensor("Wp2T", [D, D], F32, kind="ExternalInput").ap()]
    identv = nc.dram_tensor("identv", [128, ND], F32, kind="ExternalInput").ap()
    u0c = nc.dram_tensor("u0c", [128, ND * 8], F16, kind="ExternalInput").ap()
    ones128 = nc.dram_tensor("ones128", [128, 1], BF16, kind="ExternalInput").ap()
    onesk1f = nc.dram_tensor("onesk1f", [1, 128], F32, kind="ExternalInput").ap()
    negk1f = nc.dram_tensor("negk1f", [1, 128], F32, kind="ExternalInput").ap()
    onesk1b = nc.dram_tensor("onesk1b", [1, 128], BF16, kind="ExternalInput").ap()
    eye16 = nc.dram_tensor("eye16", [128, 128], F16, kind="ExternalInput").ap()
    y = nc.dram_tensor("y", [D, T], F32, kind="ExternalOutput").ap()
    pdbg = None
    if ns_only:
        pdbg = [nc.dram_tensor("p1dbg", [D, D], F32, kind="ExternalOutput").ap(),
                nc.dram_tensor("p2dbg", [D, D], F32, kind="ExternalOutput").ap()]

    qarr = nc.dram_tensor("q_scr", [D, T], BF16).ap()
    varr = nc.dram_tensor("v_scr", [D, T], BF16).ap()
    lvl = [qarr]
    for k in range(1, NLEV):
        lvl.append(nc.dram_tensor(f"lvl{k}_scr", [D, LEVELS[k]], BF16).ap())
    res = [nc.dram_tensor("res0_scr", [D, T], BF16).ap()]
    for k in range(1, NLEV):
        res.append(lvl[k] if k == NLEV - 1 else nc.dram_tensor(f"res{k}_scr", [D, LEVELS[k]], BF16).ap())

    xs = halloc(nc, "xs", [128, 2 * ND * TT], F32)
    xbf = halloc(nc, "xbf", [128, ND * TT], BF16)
    lnb = halloc(nc, "lnb", [128, ND * TT], BF16)
    pool0 = halloc(nc, "pool0", [128, 16448], BF16)
    pool1 = halloc(nc, "pool1", [128, 16448], BF16)
    mz = halloc(nc, "mz", [128, 8192], F16)
    PT = [halloc(nc, "P1T", [128, ND * D], BF16), halloc(nc, "P2T", [128, ND * D], BF16)]
    wbig = halloc(nc, "wbig", [128, ND * D], BF16)
    ubuf = halloc(nc, "ubuf", [128, 2 * ND * 8], F16)
    rowa = halloc(nc, "rowa", [1, 3 * TT], F32)
    rowbf = halloc(nc, "rowbf", [1, TT], BF16)
    facs = halloc(nc, "facs", [128, 1], F32)
    rstd_b = halloc(nc, "rstd_b", [128, TT], F32)
    nmr_b = halloc(nc, "nmr_b", [128, TT], F32)
    identf_sb = halloc(nc, "identf_sb", [128, ND], F32)
    ident_sb = halloc(nc, "ident_sb", [128, ND], BF16)
    eye16_sb = halloc(nc, "eye16_sb", [128, 128], F16)
    ones_sb = halloc(nc, "ones_sb", [128, 1], BF16)
    onesk1_sb = halloc(nc, "onesk1_sb", [1, 128], F32)
    negk1_sb = halloc(nc, "negk1_sb", [1, 128], F32)
    onesk1b_sb = halloc(nc, "onesk1b_sb", [1, 128], BF16)

    p0f = pool0.bitcast(F16)
    p1f = pool1.bitcast(F16)
    Xs = p0f[:, 0:8192]
    XT = [p0f[:, 8224:16416], p1f[:, 0:8192]]
    Amat = p1f[:, 8224:16416]
    Mmat = mz[:, :]
    zsb = mz.bitcast(F32)
    stage = pool0[:, 0:8 * PITCH]
    ostage = pool0[:, 8 * PITCH:16 * PITCH]
    astage = pool1[:, 0:4096]
    bstage = pool1[:, 4096:8192]
    rstage = pool1[:, 8192:12288]
    ebuf = pool1[:, 12288:16384]

    ps = [palloc(nc, f"ps{i}", 512) for i in range(8)]

    p = Prog(nc)

    def orchestrate(env):
        gq, te, ve, se = env.gpsimd, env.tensor, env.vector, env.scalar
        qw = DQ(p, "gpsimd", "qw")
        qx = [DQ(p, "sync", f"qx{i}") for i in range(2)]
        qst = [DQ(p, "sync", f"qst{i}") for i in range(2)]
        qc = DQ(p, "sync", "qc")
        qu = DQ(p, "sync", "qu")
        qsc = [DQ(p, "sync", f"qsc{i}") for i in range(2)]
        qsr = [DQ(p, "sync", f"qsr{i}") for i in range(2)]
        qscs = [DQ(p, "gpsimd", f"qscs{i}") for i in range(2)]
        B = p.buf
        b_w = B("w"); b_consts = B("consts")
        b_xs = [B("xs0"), B("xs1")]
        b_xbf = B("xbf"); b_ln = B("ln")
        b_rowa = B("rowa"); b_rowa2 = B("rowa2"); b_rowbf = B("rowbf")
        b_rstdb = B("rstdb"); b_nmrb = B("nmrb")
        b_pool0 = B("pool0"); b_pool1 = B("pool1"); b_mz = B("mz")
        b_X = b_pool0; b_XT = [b_pool0, b_pool1]; b_A = b_pool1; b_M = b_mz
        b_PT = [B("P1T"), B("P2T")]
        b_u = B("u"); b_fac = B("fac")
        b_stage = b_pool0; b_astage = b_pool1; b_bstage = b_pool1
        b_rstage = b_pool1; b_zsb = b_mz; b_ostage = b_pool0; b_ebuf = b_pool1
        b_arr = [B(f"arr{k}") for k in range(NLEV)]
        b_resb = [B(f"res{k}") for k in range(NLEV)]
        b_varr = B("varr")
        b_ps = [B(f"ps{i}") for i in range(8)]

        with qc.group(env, writes=[b_consts]) as g:
            g.dma(identf_sb[:, :], identv[:, :])
            g.dma(eye16_sb[:, :], eye16[:, :])
            g.dma(ones_sb[:, :], ones128[:, :])
            g.dma(onesk1_sb[:, :], onesk1f[:, :])
            g.dma(negk1_sb[:, :], negk1f[:, :])
            g.dma(onesk1b_sb[:, :], onesk1b[:, :])
        with env.group(ve, reads=[b_consts], writes=[b_consts]) as g:
            g.last = ve.tensor_copy(ident_sb[:, :], identf_sb[:, :])

        def mm_1024(b_lhs, lhs_tile, b_rhs, rhs_tile, b_out, out_tile):
            for i in range(ND):
                for jh in range(2):
                    bk = (i * 2 + jh) % 2
                    with env.group(te, reads=[b_lhs, b_rhs], writes=[b_ps[bk]]) as g:
                        for kk in range(ND):
                            g.last = te.matmul(ps[bk][:, 0:512], lhs_tile(kk, i), rhs_tile(kk, jh),
                                               start=(kk == 0), stop=(kk == ND - 1))
                    with env.group(ve, reads=[b_ps[bk]], writes=[b_out]) as g:
                        g.last = ve.tensor_copy(out_tile(i, jh), ps[bk][:, 0:512])

        def matvec_block(b_op, op_tiles, src_off, dst_off):
            for i in range(ND):
                with env.group(te, reads=[b_op, b_u], writes=[b_ps[i]]) as g:
                    for kk in range(ND):
                        g.last = te.matmul(ps[i][:, 0:8], op_tiles(kk, i),
                                           ubuf[:, src_off + kk * 8:src_off + (kk + 1) * 8],
                                           start=(kk == 0), stop=(kk == ND - 1))
            with env.group(ve, reads=[b_ps[i] for i in range(ND)], writes=[b_u]) as g:
                for i in range(ND):
                    g.last = ve.tensor_copy(ubuf[:, dst_off + i * 8:dst_off + (i + 1) * 8],
                                            ps[i][:, 0:8])

        def emit_ns(mi):
            with qw.group(env, writes=[b_X, b_XT[0]]) as g:
                for c in range(ND):
                    g.dma(Xs[:, c * D:(c + 1) * D], Wp[mi][c * 128:(c + 1) * 128, :])
                    g.dma(XT[0][:, c * D:(c + 1) * D], WpT[mi][c * 128:(c + 1) * 128, :])
            with qu.group(env, writes=[b_u]) as g:
                g.dma(ubuf[:, 0:ND * 8], u0c[:, :])

            XT0_tiles = lambda kk, i: XT[0][:, kk * D + i * 128:kk * D + (i + 1) * 128]
            Xs_tiles = lambda kk, i: Xs[:, kk * D + i * 128:kk * D + (i + 1) * 128]

            for it in range(POWER_ITERS + 1):
                src = (it % 2) * ND * 8
                dst = ((it + 1) % 2) * ND * 8
                matvec_block(b_XT[0], XT0_tiles, src, dst)
                matvec_block(b_X, Xs_tiles, dst, src)
                if it == POWER_ITERS:
                    break
                with env.group(ve, reads=[b_u], writes=[b_xbf]) as g:
                    g.last = ve.tensor_mul(xbf.bitcast(F32)[:, 0:ND * 8],
                                           ubuf[:, src:src + ND * 8], ubuf[:, src:src + ND * 8])
                with env.group(ve, reads=[b_xbf], writes=[b_xbf]) as g:
                    g.last = ve.tensor_copy(xbf[:, 2 * ND * 8:3 * ND * 8],
                                            xbf.bitcast(F32)[:, 0:ND * 8])
                with env.group(te, reads=[b_xbf, b_consts], writes=[b_ps[0]]) as g:
                    g.last = te.matmul(ps[0][0:1, 0:ND * 8], ones_sb[:, :],
                                       xbf[:, 2 * ND * 8:3 * ND * 8], start=True, stop=True)
                with env.group(ve, reads=[b_ps[0]], writes=[b_rowa]) as g:
                    g.last = ve.reduce_sum(rowa[0:1, 0:1], ps[0][0:1, 0:ND * 8], AX.X)
                with env.group(se, reads=[b_rowa], writes=[b_rowa]) as g:
                    g.last = se.activation(rowa[0:1, 1:2], rowa[0:1, 0:1], AF.Sqrt)
                with env.group(ve, reads=[b_rowa], writes=[b_rowa]) as g:
                    g.last = ve.reciprocal(rowa[0:1, 2:3], rowa[0:1, 1:2])
                with env.group(te, reads=[b_rowa, b_consts], writes=[b_ps[1]]) as g:
                    g.last = te.matmul(ps[1][:, 0:1], onesk1_sb[0:1, :], rowa[0:1, 2:3],
                                       start=True, stop=True)
                with env.group(ve, reads=[b_ps[1]], writes=[b_fac]) as g:
                    g.last = ve.tensor_copy(facs[:, 0:1], ps[1][:, 0:1])
                with env.group(ve, reads=[b_fac], writes=[b_u]) as g:
                    g.last = ve.tensor_scalar(ubuf[:, src:src + ND * 8], ubuf[:, src:src + ND * 8],
                                              facs[:, 0:1], None, ALU.mult)

            src = (POWER_ITERS % 2) * ND * 8
            dst = ((POWER_ITERS + 1) % 2) * ND * 8
            with env.group(ve, reads=[b_u], writes=[b_xbf]) as g:
                g.last = ve.tensor_mul(xbf.bitcast(F32)[:, 0:ND * 8],
                                       ubuf[:, src:src + ND * 8], ubuf[:, src:src + ND * 8])
            with env.group(ve, reads=[b_xbf], writes=[b_xbf]) as g:
                g.last = ve.tensor_copy(xbf[:, 2 * ND * 8:3 * ND * 8],
                                        xbf.bitcast(F32)[:, 0:ND * 8])
            with env.group(te, reads=[b_xbf, b_consts], writes=[b_ps[0]]) as g:
                g.last = te.matmul(ps[0][0:1, 0:ND * 8], ones_sb[:, :],
                                   xbf[:, 2 * ND * 8:3 * ND * 8], start=True, stop=True)
            with env.group(ve, reads=[b_ps[0]], writes=[b_rowa]) as g:
                g.last = ve.reduce_sum(rowa[0:1, 0:1], ps[0][0:1, 0:ND * 8], AX.X)
            with env.group(se, reads=[b_rowa], writes=[b_rowa]) as g:
                g.last = se.activation(rowa[0:1, 1:2], rowa[0:1, 0:1], AF.Sqrt)
            with env.group(ve, reads=[b_rowa], writes=[b_rowa]) as g:
                g.last = ve.reciprocal(rowa[0:1, 2:3], rowa[0:1, 1:2])
            with env.group(te, reads=[b_rowa, b_consts], writes=[b_ps[1]]) as g:
                g.last = te.matmul(ps[1][:, 0:1], onesk1_sb[0:1, :], rowa[0:1, 2:3],
                                   start=True, stop=True)
            with env.group(ve, reads=[b_ps[1]], writes=[b_fac]) as g:
                g.last = ve.tensor_copy(facs[:, 0:1], ps[1][:, 0:1])
            with env.group(ve, reads=[b_fac], writes=[b_u]) as g:
                g.last = ve.tensor_scalar(ubuf[:, src:src + ND * 8], ubuf[:, src:src + ND * 8],
                                          facs[:, 0:1], None, ALU.mult)
            matvec_block(b_XT[0], XT0_tiles, src, dst)
            with env.group(ve, reads=[b_u], writes=[b_xbf]) as g:
                ve.tensor_mul(xbf.bitcast(F32)[:, 0:ND * 8],
                              ubuf[:, dst:dst + ND * 8], ubuf[:, dst:dst + ND * 8])
                g.last = ve.tensor_mul(xbf.bitcast(F32)[:, ND * 8:2 * ND * 8],
                                       ubuf[:, src:src + ND * 8], ubuf[:, src:src + ND * 8])
            with env.group(ve, reads=[b_xbf], writes=[b_xbf]) as g:
                g.last = ve.tensor_copy(xbf[:, 4 * ND * 8:6 * ND * 8],
                                        xbf.bitcast(F32)[:, 0:2 * ND * 8])
            with env.group(te, reads=[b_xbf, b_consts], writes=[b_ps[0]]) as g:
                g.last = te.matmul(ps[0][0:1, 0:2 * ND * 8], ones_sb[:, :],
                                   xbf[:, 4 * ND * 8:6 * ND * 8], start=True, stop=True)
            with env.group(ve, reads=[b_ps[0]], writes=[b_rowa]) as g:
                g.last = ve.reduce_sum(rowa[0:1, 8:24].rearrange("p (a c) -> p a c", a=2),
                                       ps[0][0:1, 0:2 * ND * 8].rearrange("p (a k c) -> p a c k", a=2, k=ND),
                                       AX.X)
            with env.group(ve, reads=[b_rowa], writes=[b_rowa2]) as g:
                g.last = ve.reciprocal(rowa[0:1, 36:44], rowa[0:1, 16:24])
            with env.group(ve, reads=[b_rowa, b_rowa2], writes=[b_rowa2]) as g:
                g.last = ve.tensor_mul(rowa[0:1, 28:36], rowa[0:1, 8:16], rowa[0:1, 36:44])
            with env.group(ve, reads=[b_rowa2], writes=[b_rowa2]) as g:
                g.last = ve.reduce_max(rowa[0:1, 24:25], rowa[0:1, 28:36], AX.X)
            with env.group(se, reads=[b_rowa2], writes=[b_rowa2]) as g:
                g.last = se.activation(rowa[0:1, 25:26], rowa[0:1, 24:25], AF.Sqrt)
            with env.group(ve, reads=[b_rowa2], writes=[b_rowa2]) as g:
                g.last = ve.tensor_scalar(rowa[0:1, 26:27], rowa[0:1, 25:26], PRESCALE, None, ALU.mult)
            with env.group(ve, reads=[b_rowa2], writes=[b_rowa2]) as g:
                g.last = ve.reciprocal(rowa[0:1, 27:28], rowa[0:1, 26:27])
            with env.group(te, reads=[b_rowa2, b_consts], writes=[b_ps[1]]) as g:
                g.last = te.matmul(ps[1][:, 0:1], onesk1_sb[0:1, :], rowa[0:1, 27:28],
                                   start=True, stop=True)
            with env.group(ve, reads=[b_ps[1]], writes=[b_fac]) as g:
                g.last = ve.tensor_copy(facs[:, 0:1], ps[1][:, 0:1])
            with env.group(ve, reads=[b_fac], writes=[b_X]) as g:
                for c in range(ND):
                    g.last = ve.tensor_scalar(Xs[:, c * D:(c + 1) * D], Xs[:, c * D:(c + 1) * D],
                                              facs[:, 0:1], None, ALU.mult)
            with env.group(ve, reads=[b_fac], writes=[b_XT[0]]) as g:
                for c in range(ND):
                    g.last = ve.tensor_scalar(XT[0][:, c * D:(c + 1) * D], XT[0][:, c * D:(c + 1) * D],
                                              facs[:, 0:1], None, ALU.mult)

            sched = [NS_SCHED[0], NS_SCHED[0]] + NS_SCHED + [None] * NS_CUBIC
            xt_par = 0
            for coef in sched:
                bxt = b_XT[xt_par]
                xtb = XT[xt_par]
                mm_1024(b_X, Xs_tiles,
                        b_X, lambda kk, jh: Xs[:, kk * D + jh * 512:kk * D + (jh + 1) * 512],
                        b_A, lambda i, jh: Amat[:, i * D + jh * 512:i * D + (jh + 1) * 512])
                if coef is not None:
                    a, bq_, cq = coef
                    mm_1024(b_A, lambda kk, i: Amat[:, kk * D + i * 128:kk * D + (i + 1) * 128],
                            b_A, lambda kk, jh: Amat[:, kk * D + jh * 512:kk * D + (jh + 1) * 512],
                            b_M, lambda i, jh: Mmat[:, i * D + jh * 512:i * D + (jh + 1) * 512])
                    with env.group(ve, reads=[b_A], writes=[b_M]) as g:
                        for c in range(ND):
                            g.last = ve.tensor_scalar(Mmat[:, c * D:(c + 1) * D],
                                                      Mmat[:, c * D:(c + 1) * D], cq, None, ALU.mult)
                    with env.group(ve, reads=[b_A], writes=[b_M]) as g:
                        for c in range(ND):
                            g.last = ve.scalar_tensor_tensor(
                                Mmat[:, c * D:(c + 1) * D], Amat[:, c * D:(c + 1) * D],
                                bq_, Mmat[:, c * D:(c + 1) * D], ALU.mult, ALU.add)
                    diag_a = a
                else:
                    with env.group(ve, reads=[b_A], writes=[b_M]) as g:
                        for c in range(ND):
                            g.last = ve.tensor_scalar(Mmat[:, c * D:(c + 1) * D],
                                                      Amat[:, c * D:(c + 1) * D], -0.5, None, ALU.mult)
                    diag_a = 1.5
                with env.group(ve, reads=[b_consts], writes=[b_M]) as g:
                    for c in range(ND):
                        g.last = ve.scalar_tensor_tensor(
                            Mmat[:, c * D + c * 128:c * D + (c + 1) * 128], eye16_sb[:, :],
                            diag_a, Mmat[:, c * D + c * 128:c * D + (c + 1) * 128], ALU.mult, ALU.add)
                mm_1024(bxt, lambda kk, i: xtb[:, kk * D + i * 128:kk * D + (i + 1) * 128],
                        b_M, lambda kk, jh: Mmat[:, kk * D + jh * 512:kk * D + (jh + 1) * 512],
                        b_X, lambda i, jh: Xs[:, i * D + jh * 512:i * D + (jh + 1) * 512])
                mm_1024(b_M, lambda kk, i: Mmat[:, kk * D + i * 128:kk * D + (i + 1) * 128],
                        bxt, lambda kk, jh: xtb[:, kk * D + jh * 512:kk * D + (jh + 1) * 512],
                        b_XT[1 - xt_par],
                        lambda i, jh, t=XT[1 - xt_par]: t[:, i * D + jh * 512:i * D + (jh + 1) * 512])
                xt_par = 1 - xt_par
            with env.group(ve, reads=[b_XT[xt_par]], writes=[b_PT[mi]]) as g:
                for c in range(ND):
                    g.last = ve.tensor_copy(PT[mi][:, c * D:(c + 1) * D],
                                            XT[xt_par][:, c * D:(c + 1) * D])
            if ns_only:
                for half in range(2):
                    with env.group(ve, reads=[b_X], writes=[b_zsb]) as g:
                        for c in range(ND):
                            g.last = ve.tensor_copy(
                                zsb[:, c * TT:(c + 1) * TT],
                                Xs[:, c * D + half * 512:c * D + half * 512 + 512])
                    with qst[half].group(env, reads=[b_zsb]) as g:
                        for c in range(ND):
                            g.dma(pdbg[mi][c * 128:(c + 1) * 128, half * 512:half * 512 + 512],
                                  zsb[:, c * TT:(c + 1) * TT])

        emit_ns(0)
        emit_ns(1)
        if ns_only:
            env.wait_tok(env.sync, (p.sem("qst0"), p.sem("qst0").n))
            env.wait_tok(env.sync, (p.sem("qst1"), p.sem("qst1").n))
            return

        ns_done = [(p.sem("pe"), p.sem("pe").n), (p.sem("ve"), p.sem("ve").n)]

        for (wsrc, dstarr, b_dstarr) in ((WqT, qarr, b_arr[0]), (WvT, varr, b_varr)):
            with qw.group(env, writes=[b_w], extra=ns_done) as g:
                for c in range(ND):
                    g.dma(wbig[:, c * D:(c + 1) * D], wsrc[c * 128:(c + 1) * 128, :])
            for k in range(NTT):
                slot = k % 2
                xo = slot * ND * TT
                t0 = k * TT
                with qx[slot].group(env, writes=[b_xs[slot]]) as g:
                    for gi in range(ND):
                        g.dma(xs[:, xo + gi * TT:xo + (gi + 1) * TT],
                              x[gi * 128:(gi + 1) * 128, t0:t0 + TT])
                b = {"src": b_xs[slot], "xbf": b_xbf, "rowa": b_rowa, "rowa2": b_rowa2,
                     "rstd_b": b_rstdb, "nmr_b": b_nmrb, "dst": b_ln, "consts": b_consts,
                     "ps0": b_ps[0], "ps1": b_ps[1], "ps2": b_ps[2], "ps3": b_ps[3]}
                aps = {"xbf": xbf, "rowa": rowa, "rstd_b": rstd_b, "nmr_b": nmr_b,
                       "ones_sb": ones_sb, "onesk1_sb": onesk1_sb, "negk1_sb": negk1_sb, "ps": ps}
                emit_ln_generic(env, te, ve, se, gq, b, aps,
                                lambda gi: xs[:, xo + gi * TT:xo + (gi + 1) * TT],
                                lambda gi: lnb[:, gi * TT:(gi + 1) * TT])
                for go in range(ND):
                    bk = 4 + (go % 2)
                    with env.group(te, reads=[b_ln, b_w], writes=[b_ps[bk]]) as g:
                        for c in range(ND):
                            g.last = te.matmul(ps[bk][:, 0:TT],
                                               wbig[:, c * D + go * 128:c * D + (go + 1) * 128],
                                               lnb[:, c * TT:(c + 1) * TT],
                                               start=(c == 0), stop=(c == ND - 1))
                    with env.group(ve, reads=[b_ps[bk]], writes=[b_ebuf], extra=ns_done) as g:
                        g.last = ve.tensor_copy(ebuf[:, go * TT // 1:go * 512 + 512][:, 0:512]
                                                if False else ebuf[:, go * 512:go * 512 + 512],
                                                ps[bk][:, 0:TT])
                with qscs[slot].group(env, reads=[b_ebuf], writes=[b_dstarr]) as g:
                    for gi in range(ND):
                        g.dma(dstarr[gi * 128:(gi + 1) * 128, t0:t0 + TT],
                              ebuf[:, gi * 512:gi * 512 + 512])

        def combine_tiles(b_aop, a_tile, b_bop, b_tile, nout):
            for i in range(ND):
                bk = 4 + i % 2
                with env.group(te, reads=[b_aop, b_bop, b_PT[0], b_PT[1]], writes=[b_ps[bk]]) as g:
                    for kk in range(ND):
                        te.matmul(ps[bk][:, 0:nout],
                                  PT[0][:, kk * D + i * 128:kk * D + (i + 1) * 128],
                                  a_tile(kk), start=(kk == 0), stop=False)
                    for kk in range(ND):
                        g.last = te.matmul(ps[bk][:, 0:nout],
                                           PT[1][:, kk * D + i * 128:kk * D + (i + 1) * 128],
                                           b_tile(kk), start=False, stop=(kk == ND - 1))
                with env.group(ve, reads=[b_ps[bk]], writes=[b_zsb]) as g:
                    g.last = ve.tensor_copy(zsb[:, i * TT:i * TT + nout], ps[bk][:, 0:nout])
            with env.group(ve, reads=[b_zsb], writes=[b_xbf]) as g:
                for i in range(ND):
                    g.last = ve.tensor_mul(xbf[:, i * TT:i * TT + nout],
                                           zsb[:, i * TT:i * TT + nout],
                                           zsb[:, i * TT:i * TT + nout])
            with env.group(te, reads=[b_xbf, b_consts], writes=[b_ps[2]]) as g:
                for i in range(ND):
                    g.last = te.matmul(ps[2][0:1, 0:nout], ones_sb[:, :],
                                       xbf[:, i * TT:i * TT + nout],
                                       start=(i == 0), stop=(i == ND - 1))
            with env.group(ve, reads=[b_ps[2]], writes=[b_rowa]) as g:
                g.last = ve.tensor_scalar(rowa[0:1, 0:nout], ps[2][0:1, 0:nout],
                                          1.0 / D, RMS_EPS, ALU.mult, ALU.add)
            with env.group(se, reads=[b_rowa], writes=[b_rowa]) as g:
                g.last = se.activation(rowa[0:1, TT:TT + nout], rowa[0:1, 0:nout], AF.Sqrt)
            with env.group(ve, reads=[b_rowa], writes=[b_rowa]) as g:
                g.last = ve.reciprocal(rowa[0:1, 2 * TT:2 * TT + nout], rowa[0:1, TT:TT + nout])
            with env.group(ve, reads=[b_rowa], writes=[b_rowbf]) as g:
                g.last = ve.tensor_copy(rowbf[0:1, 0:nout], rowa[0:1, 2 * TT:2 * TT + nout])
            with env.group(te, reads=[b_rowbf, b_consts], writes=[b_ps[3]]) as g:
                g.last = te.matmul(ps[3][:, 0:nout], onesk1b_sb[0:1, :],
                                   rowbf[0:1, 0:nout], start=True, stop=True)
            with env.group(ve, reads=[b_ps[3]], writes=[b_rstdb]) as g:
                g.last = ve.tensor_copy(rstd_b[:, 0:nout], ps[3][:, 0:nout])

        def deinterleave(dst, b_dst, par, n, off=0):
            with env.group(ve, reads=[b_stage], writes=[b_dst]) as g:
                for gi in range(ND):
                    vv = stage[:, gi * PITCH + off:gi * PITCH + off + 2 * n]
                    g.last = ve.tensor_copy(
                        dst[:, gi * 512:gi * 512 + n],
                        vv.rearrange("p (c two) -> p two c", two=2)[:, par, :])

        for k in range(NLEV - 1):
            m = LEVELS[k]
            nout = m // 2
            for j0 in range(0, nout, TT):
                tw = min(TT, nout - j0)
                sslot = (j0 // TT) % 2
                with qsc[sslot].group(env, reads=[b_arr[k]], writes=[b_stage], extra=ns_done) as g:
                    if k == 0:
                        if j0 == 0:
                            for gi in range(ND):
                                g.dma(stage[:, gi * PITCH + 1:gi * PITCH + 2 * tw],
                                      qarr[gi * 128:(gi + 1) * 128, 0:2 * tw - 1])
                        else:
                            for gi in range(ND):
                                g.dma(stage[:, gi * PITCH:gi * PITCH + 2 * tw],
                                      qarr[gi * 128:(gi + 1) * 128, 2 * j0 - 1:2 * j0 - 1 + 2 * tw])
                    else:
                        for gi in range(ND):
                            g.dma(stage[:, gi * PITCH:gi * PITCH + 2 * tw],
                                  lvl[k][gi * 128:(gi + 1) * 128, 2 * j0:2 * j0 + 2 * tw])
                if k == 0 and j0 == 0:
                    with env.group(ve, reads=[b_consts], writes=[b_stage]) as g:
                        for gi in range(ND):
                            g.last = ve.tensor_copy(stage[:, gi * PITCH:gi * PITCH + 1],
                                                    ident_sb[:, gi:gi + 1])
                deinterleave(astage, b_astage, 0, tw)
                deinterleave(bstage, b_bstage, 1, tw)
                combine_tiles(b_astage, lambda kk: astage[:, kk * 512:kk * 512 + tw],
                              b_bstage, lambda kk: bstage[:, kk * 512:kk * 512 + tw], tw)
                with env.group(ve, reads=[b_zsb, b_rstdb], writes=[b_ebuf]) as g:
                    for gi in range(ND):
                        g.last = ve.tensor_tensor(ebuf[:, gi * 512:gi * 512 + tw],
                                                  zsb[:, gi * TT:gi * TT + tw],
                                                  rstd_b[:, 0:tw], ALU.mult)
                with qscs[sslot].group(env, reads=[b_ebuf], writes=[b_arr[k + 1]]) as g:
                    for gi in range(ND):
                        g.dma(lvl[k + 1][gi * 128:(gi + 1) * 128, j0:j0 + tw],
                              ebuf[:, gi * 512:gi * 512 + tw])

        for k in range(NLEV - 2, -1, -1):
            m = LEVELS[k]
            nred = m // 2
            n_even = nred if m % 2 == 1 else nred - 1
            for j0 in range(0, nred, TT):
                tw = min(TT, nred - j0)
                ew = min(tw, max(0, n_even - j0))
                sslot = (j0 // TT) % 2
                with qsr[sslot].group(env, reads=[b_resb[k + 1]], writes=[b_rstage]) as g:
                    for gi in range(ND):
                        g.dma(rstage[:, gi * 512:gi * 512 + tw],
                              res[k + 1][gi * 128:(gi + 1) * 128, j0:j0 + tw])
                need_col0 = (k > 0 and j0 == 0)
                if ew > 0 or need_col0:
                    with qsc[sslot].group(env, reads=[b_arr[k]], writes=[b_stage]) as g:
                        for gi in range(ND):
                            if k == 0:
                                g.dma(stage[:, gi * PITCH:gi * PITCH + 2 * ew - 1],
                                      qarr[gi * 128:(gi + 1) * 128, 2 * j0 + 1:2 * j0 + 2 * ew])
                            elif j0 == 0:
                                g.dma(stage[:, gi * PITCH:gi * PITCH + 2 * ew + 2],
                                      lvl[k][gi * 128:(gi + 1) * 128, 0:2 * ew + 2])
                            else:
                                g.dma(stage[:, gi * PITCH:gi * PITCH + 2 * ew],
                                      lvl[k][gi * 128:(gi + 1) * 128, 2 * j0 + 2:2 * j0 + 2 + 2 * ew])
                if ew > 0:
                    boff = 2 if need_col0 else 0
                    deinterleave(bstage, b_bstage, 0, ew, off=boff)
                    combine_tiles(b_rstage, lambda kk: rstage[:, kk * 512:kk * 512 + ew],
                                  b_bstage, lambda kk: bstage[:, kk * 512:kk * 512 + ew], ew)
                    with env.group(ve, reads=[b_zsb, b_rstdb], writes=[b_ebuf]) as g:
                        for gi in range(ND):
                            g.last = ve.tensor_tensor(ebuf[:, gi * 512:gi * 512 + ew],
                                                      zsb[:, gi * TT:gi * TT + ew],
                                                      rstd_b[:, 0:ew], ALU.mult)
                if k == 0:
                    with env.group(ve, reads=[b_rstage], writes=[b_ostage]) as g:
                        for gi in range(ND):
                            vv = ostage[:, gi * PITCH:gi * PITCH + 2 * tw]
                            g.last = ve.tensor_copy(
                                vv.rearrange("p (c two) -> p two c", two=2)[:, 0, :],
                                rstage[:, gi * 512:gi * 512 + tw])
                    with env.group(ve, reads=[b_ebuf], writes=[b_ostage]) as g:
                        for gi in range(ND):
                            vv = ostage[:, gi * PITCH:gi * PITCH + 2 * tw]
                            g.last = ve.tensor_copy(
                                vv.rearrange("p (c two) -> p two c", two=2)[:, 1, :],
                                ebuf[:, gi * 512:gi * 512 + tw])
                    with qscs[sslot].group(env, reads=[b_ostage], writes=[b_resb[0]]) as g:
                        for gi in range(ND):
                            g.dma(res[0][gi * 128:(gi + 1) * 128, 2 * j0:2 * j0 + 2 * tw],
                                  ostage[:, gi * PITCH:gi * PITCH + 2 * tw])
                else:
                    ncols = (1 if need_col0 else 0) + tw + ew
                    cbase = 0 if need_col0 else 2 * j0 + 1
                    with env.group(ve, reads=[b_rstage, b_stage], writes=[b_ostage]) as g:
                        for gi in range(ND):
                            off = gi * PITCH
                            o0 = 1 if need_col0 else 0
                            vv = ostage[:, off + o0:off + o0 + 2 * tw]
                            g.last = ve.tensor_copy(
                                vv.rearrange("p (c two) -> p two c", two=2)[:, 0, :],
                                rstage[:, gi * 512:gi * 512 + tw])
                    if ew > 0:
                        with env.group(ve, reads=[b_ebuf], writes=[b_ostage]) as g:
                            for gi in range(ND):
                                off = gi * PITCH
                                o0 = (1 if need_col0 else 0) + 1
                                vv = ostage[:, off + o0:off + o0 + 2 * ew]
                                g.last = ve.tensor_copy(
                                    vv.rearrange("p (c two) -> p two c", two=2)[:, 0, :],
                                    ebuf[:, gi * 512:gi * 512 + ew])
                    if need_col0:
                        with env.group(ve, reads=[b_stage], writes=[b_ostage]) as g:
                            for gi in range(ND):
                                g.last = ve.tensor_copy(ostage[:, gi * PITCH:gi * PITCH + 1],
                                                        stage[:, gi * PITCH:gi * PITCH + 1])
                    with qscs[sslot].group(env, reads=[b_ostage], writes=[b_resb[k]]) as g:
                        for gi in range(ND):
                            g.dma(res[k][gi * 128:(gi + 1) * 128, cbase:cbase + ncols],
                                  ostage[:, gi * PITCH:gi * PITCH + ncols])

        with qw.group(env, writes=[b_w]) as g:
            for c in range(ND):
                g.dma(wbig[:, c * D:(c + 1) * D], WcT[c * 128:(c + 1) * 128, :])
        for k in range(NTT):
            slot = k % 2
            xo = slot * ND * TT
            t0 = k * TT
            with qx[slot].group(env, writes=[b_xs[slot]]) as g:
                for gi in range(ND):
                    g.dma(xs[:, xo + gi * TT:xo + (gi + 1) * TT],
                          x[gi * 128:(gi + 1) * 128, t0:t0 + TT])
            with qsc[slot].group(env, reads=[b_resb[0]], writes=[b_astage]) as g:
                for gi in range(ND):
                    g.dma(astage[:, gi * 512:(gi + 1) * 512],
                          res[0][gi * 128:(gi + 1) * 128, t0:t0 + TT])
            with qsc[slot].group(env, reads=[b_varr], writes=[b_bstage]) as g:
                for gi in range(ND):
                    g.dma(bstage[:, gi * 512:(gi + 1) * 512],
                          varr[gi * 128:(gi + 1) * 128, t0:t0 + TT])
            with env.group(ve, reads=[b_astage, b_bstage], writes=[b_xbf]) as g:
                for gi in range(ND):
                    g.last = ve.tensor_mul(xbf[:, gi * TT:(gi + 1) * TT],
                                           astage[:, gi * 512:(gi + 1) * 512],
                                           bstage[:, gi * 512:(gi + 1) * 512])
            for go in range(ND):
                bk = 4 + (go % 2)
                with env.group(te, reads=[b_xbf, b_w], writes=[b_ps[bk]]) as g:
                    for c in range(ND):
                        g.last = te.matmul(ps[bk][:, 0:TT],
                                           wbig[:, c * D + go * 128:c * D + (go + 1) * 128],
                                           xbf[:, c * TT:(c + 1) * TT],
                                           start=(c == 0), stop=(c == ND - 1))
                with env.group(ve, reads=[b_ps[bk]], writes=[b_xs[slot]]) as g:
                    g.last = ve.tensor_tensor(xs[:, xo + go * TT:xo + (go + 1) * TT],
                                              xs[:, xo + go * TT:xo + (go + 1) * TT],
                                              ps[bk][:, 0:TT], ALU.add)
            with qst[slot].group(env, reads=[b_xs[slot]]) as g:
                for gi in range(ND):
                    g.dma(y[gi * 128:(gi + 1) * 128, t0:t0 + TT],
                          xs[:, xo + gi * TT:xo + (gi + 1) * TT])

        env.wait_tok(env.sync, (p.sem("qst0"), p.sem("qst0").n))
        env.wait_tok(env.sync, (p.sem("qst1"), p.sem("qst1").n))

    counts = p.run(orchestrate)
    return nc, counts


def host_inputs_l1attn(xT, Wq, Wv, Wc, Wp1, Wp2, identity):
    import ml_dtypes
    rng = np.random.default_rng(1234)
    return {
        "x": np.ascontiguousarray(xT, np.float32),
        "WqT": np.ascontiguousarray(np.asarray(Wq).T, np.float32),
        "WvT": np.ascontiguousarray(np.asarray(Wv).T, np.float32),
        "WcT": np.ascontiguousarray(np.asarray(Wc).T, np.float32),
        "Wp1": np.ascontiguousarray(Wp1, np.float32),
        "Wp1T": np.ascontiguousarray(np.asarray(Wp1).T, np.float32),
        "Wp2": np.ascontiguousarray(Wp2, np.float32),
        "Wp2T": np.ascontiguousarray(np.asarray(Wp2).T, np.float32),
        "identv": np.ascontiguousarray(np.asarray(identity, np.float32).reshape(ND, 128).T),
        "u0c": (rng.standard_normal((128, ND * 8)) * 0.03).astype(np.float16),
        "ones128": np.ones((128, 1), ml_dtypes.bfloat16),
        "onesk1f": np.ones((1, 128), np.float32),
        "negk1f": np.full((1, 128), -1.0, np.float32),
        "onesk1b": np.ones((1, 128), ml_dtypes.bfloat16),
        "eye16": np.eye(128, dtype=ml_dtypes.float16 if hasattr(ml_dtypes, 'float16') else np.float16),
    }



TT = 512
NTT = T // TT
from concourse.bass_utils import run_bass_kernel_spmd

_PROGS = {}

def _get(name, builder):
    if name not in _PROGS:
        _PROGS[name] = builder()
    return _PROGS[name]


def kernel(**inputs):
    NB = 8
    x0 = np.asarray(inputs["L0_attn_x"], np.float32)
    assert x0.shape == (NB, T, D)
    cores = list(range(NB))

    nc1, _ = _get("l0attn", build_l0attn)
    base1 = host_inputs_l0attn(
        np.zeros((D, T), np.float32),
        inputs["L0_attn_Wq"], inputs["L0_attn_bq"], inputs["L0_attn_Wk"], inputs["L0_attn_bk"],
        inputs["L0_attn_Wv"], inputs["L0_attn_bv"], inputs["L0_attn_Wc"], inputs["L0_attn_We"],
        inputs["L0_attn_Wo"], inputs["L0_attn_bo"], inputs["L0_attn_gamma"], inputs["L0_attn_beta"])
    in_maps = []
    for c in cores:
        m = dict(base1)
        m["x"] = np.ascontiguousarray(x0[c].T)
        in_maps.append(m)
    r = run_bass_kernel_spmd(nc1, in_maps, core_ids=cores)
    x1 = [r.results[c]["y"] for c in cores]

    nc2, _ = _get("l0ffn", build_l0ffn)
    base2 = host_inputs_l0ffn(
        np.zeros((D, T), np.float32),
        inputs["L0_ffn_W_r"], inputs["L0_ffn_W_k"], inputs["L0_ffn_W_v"],
        inputs["L0_ffn_mu_r"], inputs["L0_ffn_mu_k"])
    in_maps = []
    for c in cores:
        m = dict(base2)
        m["x"] = x1[c]
        in_maps.append(m)
    r = run_bass_kernel_spmd(nc2, in_maps, core_ids=cores)
    x2 = [r.results[c]["y"] for c in cores]

    nc3, _ = _get("l1attn", build_l1attn)
    base3 = host_inputs_l1attn(
        np.zeros((D, T), np.float32),
        inputs["L1_attn_Wq"], inputs["L1_attn_Wv"], inputs["L1_attn_Wc"],
        inputs["L1_attn_Wp1"], inputs["L1_attn_Wp2"], inputs["L1_attn_identity"])
    in_maps = []
    for c in cores:
        m = dict(base3)
        m["x"] = x2[c]
        in_maps.append(m)
    r = run_bass_kernel_spmd(nc3, in_maps, core_ids=cores)
    x3 = [r.results[c]["y"] for c in cores]

    nc4, _ = _get("moe", build_moe)
    base4 = {
        "sw1T": np.ascontiguousarray(np.asarray(inputs["L1_ffn_shared_w1"], np.float32).T),
        "sw2T": np.ascontiguousarray(np.asarray(inputs["L1_ffn_shared_w2"], np.float32).T),
        "ew1T": np.ascontiguousarray(np.asarray(inputs["L1_ffn_experts_w1"], np.float32).transpose(0, 2, 1)),
        "ew2T": np.ascontiguousarray(np.asarray(inputs["L1_ffn_experts_w2"], np.float32).transpose(0, 2, 1)),
        "gwT": np.ascontiguousarray(np.asarray(inputs["L1_ffn_gate_w"], np.float32).T),
    }
    import ml_dtypes
    base4.update({
        "ones128": np.ones((128, 1), ml_dtypes.bfloat16),
        "onesk1f": np.ones((1, 128), np.float32),
        "negk1f": np.full((1, 128), -1.0, np.float32),
        "onesk1b": np.ones((1, 128), ml_dtypes.bfloat16),
        "eye128f": np.eye(128, dtype=np.float32),
    })
    in_maps = []
    for c in cores:
        m = dict(base4)
        m["x"] = x3[c]
        in_maps.append(m)
    r = run_bass_kernel_spmd(nc4, in_maps, core_ids=cores)
    out = np.stack([np.asarray(r.results[c]["y"]).T for c in cores])
    return np.ascontiguousarray(out.astype(np.float32))
